# version 02
# baseline (speedup 1.0000x reference)
import numpy as np
from contextlib import ExitStack

import concourse.bass as bass
import concourse.mybir as mybir
from concourse.bass_utils import run_bass_kernel_spmd
from concourse.alu_op_type import AluOpType as op

F32 = mybir.dt.float32
BF16 = mybir.dt.bfloat16
AF = mybir.ActivationFunctionType
AX = mybir.AxisListType.X

D = 2048
H = 16
C = 128
QKV_N = 3 * D
MAGIC = float(3 * 2 ** 22)
NEG = -60000.0
EPS = 1e-5
L2EPS = 1e-6
LN_HALF_DK = float(-0.5 * np.log(128.0))
LN2 = float(np.log(2.0))

ENGINES = ("pe", "act", "dve", "sp", "pool")
BLOCK_NAME = dict(pe="tensor", act="scalar", dve="vector", sp="sync", pool="gpsimd")


class P:

    NLANES = 4
    CHANNELS = ("pe", "act", "dve", "sp", "pool") + tuple(
        f"{e}_d{l}" for e in ("sp", "pool") for l in range(4))

    def __init__(self, nc, es):
        self.nc = nc
        self.sem = {ch: es.enter_context(nc.semaphore(f"sem_{ch}"))
                    for ch in self.CHANNELS}
        self.q = {e: [] for e in ENGINES}
        self.cnt = {ch: 0 for ch in self.CHANNELS}
        self.lane = {"sp": 0, "pool": 0}
        self.acc = {e: ([], []) for e in ENGINES}

    @staticmethod
    def _eng_of(ch):
        return ch.split("_")[0]

    @staticmethod
    def _ext(ap):
        if ap is None or isinstance(ap, (int, float)):
            return None
        try:
            lo = ap.offset
            hi = lo + 1
            for step, count in list(ap.ap)[1:]:
                hi += step * (count - 1)
            return (ap.tensor.name, lo, hi)
        except Exception:
            return (getattr(getattr(ap, "tensor", None), "name", "?"), 0, 1 << 60)

    @staticmethod
    def _hits(x, y):
        if x[0] == "__any__" or y[0] == "__any__":
            return True
        return x[0] == y[0] and x[1] < y[2] and y[1] < x[2]

    def _conflict(self, e, writes, reads):
        tw, tr = self.acc[e]
        for w in writes:
            if w is not None and any(self._hits(x, w) for x in tw + tr):
                return True
        for r in reads:
            if r is not None and any(self._hits(x, r) for x in tw):
                return True
        return False

    def _emit(self, e, fn, deps, inc, ch=None, drain=None, w=(), r=()):
        ch = ch or e
        if drain is None and e in ("act", "dve"):
            if w or r:
                writes = [self._ext(a) for a in w]
                reads = [self._ext(a) for a in r]
            else:
                writes = [("__any__", 0, 1 << 60)]
                reads = []
            drain = self._conflict(e, writes, reads) if (w or r) else True
            if drain:
                self.acc[e] = ([], [])
            self.acc[e][0].extend(x for x in writes if x)
            self.acc[e][1].extend(x for x in reads if x)
        if drain:
            self.q[e].append((lambda g: g.drain(), (), 0, ch))
        deps = tuple(d for d in deps
                     if d is not None and (self._eng_of(d[0]) != e or "_d" in d[0]))
        self.q[e].append((fn, deps, inc, ch))
        if inc:
            self.cnt[ch] += inc
            return (ch, self.cnt[ch])
        return None

    def raw(self, e, fn, deps=(), fence=True):
        return self._emit(e, fn, deps, 1 if fence else 0)

    def nopf(self, e, deps=()):
        if e in ("pe", "act", "dve"):
            return self._emit(e, lambda g: g.drain(), deps, 1, drain=False)
        return self._emit(e, lambda g: g.nop(), deps, 1, drain=False)

    def wait_self_dma(self, e, tok):
        sem = self.sem[tok[0]]
        v = tok[1]
        self.q[e].append((lambda g, sem=sem, v=v: g.wait_ge(sem, v) or None, (), 0, e))

    def dma_quiesce(self, e):
        deps = [(f"{e}_d{l}", self.cnt[f"{e}_d{l}"]) for l in range(self.NLANES)]
        return self.nopf(e, deps=deps)

    def barrier(self):
        t1 = [self.dma_quiesce(e) if e in ("sp", "pool") else self.nopf(e)
              for e in ENGINES]
        return [self.nopf(e, deps=t1) for e in ENGINES]

    def dma(self, e, outp, inp, deps=()):
        l = self.lane[e]
        self.lane[e] = (l + 1) % self.NLANES
        ch = f"{e}_d{l}"
        gate = (ch, self.cnt[ch])
        return self._emit(e, lambda g, o=outp, i=inp: g.dma_start(out=o, in_=i),
                          list(deps) + [gate], 16, ch=ch)

    def mm(self, dst, lhs, rhs, start, stop, deps=(), fence=False):
        return self._emit("pe", lambda t, d=dst, l=lhs, r=rhs, a=start, b=stop:
                          t.matmul(d, l, r, start=a, stop=b), deps,
                          1 if fence else 0)

    def tr(self, dst, src, ident, deps=(), fence=True):
        return self._emit("pe", lambda t, d=dst, s=src, i=ident:
                          t.transpose(d, s, i), deps, 1 if fence else 0)

    def cp(self, e, dst, src, deps=(), fence=True):
        return self._emit(e, (lambda v, d=dst, s=src: v.tensor_copy(d, s)) if e == "dve"
                          else (lambda s_, d=dst, s=src: s_.activation(
                              d, s, AF.Copy, bias=0.0, scale=1.0)),
                          deps, 1 if fence else 0, w=(dst,), r=(src,))

    def ts(self, dst, src, s1, s2, o1, o2=None, deps=(), fence=True, accum=None):
        def fn(v, d=dst, s=src, s1=s1, s2=s2, o1=o1, o2=o2, ac=accum):
            if o2 is None:
                return v.tensor_scalar(d, s, s1, None, o1, accum_out=ac) \
                    if ac is not None else v.tensor_scalar(d, s, s1, None, o1)
            return v.tensor_scalar(d, s, s1, s2, o1, o2, accum_out=ac) \
                if ac is not None else v.tensor_scalar(d, s, s1, s2, o1, o2)
        aps = lambda z: z if hasattr(z, "offset") else None
        return self._emit("dve", fn, deps, 1 if fence else 0,
                          w=(dst, accum), r=(src, aps(s1), aps(s2)))

    def tt(self, dst, a, b, o, deps=(), fence=True):
        return self._emit("dve", lambda v, d=dst, a=a, b=b, o=o:
                          v.tensor_tensor(d, a, b, o), deps, 1 if fence else 0,
                          w=(dst,), r=(a, b))

    def stt(self, dst, in0, sc, in1, o1, o2, deps=(), fence=True, accum=None):
        def fn(v, d=dst, i0=in0, sc=sc, i1=in1, o1=o1, o2=o2, ac=accum):
            if ac is not None:
                return v.scalar_tensor_tensor(d, i0, sc, i1, o1, o2, accum_out=ac)
            return v.scalar_tensor_tensor(d, i0, sc, i1, o1, o2)
        aps = lambda z: z if hasattr(z, "offset") else None
        return self._emit("dve", fn, deps, 1 if fence else 0,
                          w=(dst, accum), r=(in0, in1, aps(sc)))

    def act(self, dst, src, func, bias=0.0, scale=1.0, deps=(), fence=True, accum=None):
        def fn(s_, d=dst, s=src, f=func, b=bias, sc=scale, ac=accum):
            if ac is not None:
                return s_.activation(d, s, f, bias=b, scale=sc, accum_out=ac)
            return s_.activation(d, s, f, bias=b, scale=sc)
        aps = lambda z: z if hasattr(z, "offset") else None
        return self._emit("act", fn, deps, 1 if fence else 0,
                          w=(dst, accum), r=(src, aps(bias), aps(scale)))

    def red(self, dst, src, o, absv=False, deps=(), fence=True):
        return self._emit("dve", lambda v, d=dst, s=src, o=o, a=absv:
                          v.tensor_reduce(d, s, AX, o, apply_absolute_value=a),
                          deps, 1 if fence else 0, w=(dst,), r=(src,))

    def mset(self, e, ap, val, deps=(), fence=True):
        return self._emit(e, lambda g, a=ap, v=val: g.memset(a, v), deps,
                          1 if fence else 0, w=(ap,))

    def replay(self):
        with self.nc.Block() as block:
            for e in ENGINES:
                deco = getattr(block, BLOCK_NAME[e])
                items = self.q[e]
                sems = self.sem

                def body(eng, items=items):
                    last = {}
                    for fn, deps, inc, ch in items:
                        for (pch, v) in deps:
                            if last.get(pch, 0) < v:
                                eng.wait_ge(sems[pch], v)
                                last[pch] = v
                        inst = fn(eng)
                        if inc:
                            inst.then_inc(sems[ch], inc)
                deco(body)


def build_program(T=4096, dbg=False):
    nc = bass.Bass("TRN2", target_bir_lowering=False)
    NCH = T // C
    T1 = min(2048, T)
    NT1 = T // T1
    T3 = min(512, T)
    NT3 = T // T3
    kdbg = "ExternalOutput" if dbg else "Internal"

    x_in = nc.dram_tensor("x", [T, D], F32, kind="ExternalInput").ap()
    w_qkv = nc.dram_tensor("w_qkv", [D, QKV_N], F32, kind="ExternalInput").ap()
    w_ab = nc.dram_tensor("w_ab", [D, 2 * H + D], F32, kind="ExternalInput").ap()
    conv_w = nc.dram_tensor("conv_w", [QKV_N, 4], F32, kind="ExternalInput").ap()
    a_log = nc.dram_tensor("a_log", [H, 1], F32, kind="ExternalInput").ap()
    dt_bias = nc.dram_tensor("dt_bias", [H, 1], F32, kind="ExternalInput").ap()
    w_o = nc.dram_tensor("w_o", [D, D], F32, kind="ExternalInput").ap()
    ffn_w = nc.dram_tensor("ffn_w", [D, D], F32, kind="ExternalInput").ap()
    c_ident = nc.dram_tensor("c_ident", [128, 128], F32, kind="ExternalInput").ap()
    c_masks = nc.dram_tensor("c_masks", [128, 3 * 1024], F32, kind="ExternalInput").ap()
    c_ind = nc.dram_tensor("c_ind", [48, 3 * D], F32, kind="ExternalInput").ap()
    c_oh = nc.dram_tensor("c_oh", [128, H * H], F32, kind="ExternalInput").ap()
    c_wtile = nc.dram_tensor("c_wtile", [128, D], F32, kind="ExternalInput").ap()
    c_btile = nc.dram_tensor("c_btile", [128, D], F32, kind="ExternalInput").ap()
    y_out = nc.dram_tensor("y", [T, D], F32, kind="ExternalOutput").ap()

    qkc_s = nc.dram_tensor("qkc_s", [NCH, 48, 128, 128], BF16, kind=kdbg).ap()
    sg_s = nc.dram_tensor("sg_s", [T, D], BF16, kind=kdbg).ap()
    of_s = nc.dram_tensor("of_s", [NCH, H, 128, 128], BF16, kind=kdbg).ap()
    tern_s = nc.dram_tensor("tern_s", [D, D], BF16, kind=kdbg).ap()
    if dbg:
        gH_d = nc.dram_tensor("gH_d", [H, T], F32, kind=kdbg).ap()
        lnbH_d = nc.dram_tensor("lnbH_d", [H, T], F32, kind=kdbg).ap()
        alpha_d = nc.dram_tensor("alpha_d", [1, 4], F32, kind=kdbg).ap()

    es = ExitStack()
    SG = es.enter_context
    p = P(nc, es)
    es12 = ExitStack()
    SG12 = es12.enter_context

    ident_b = SG(nc.sbuf_tensor("ident_b", [128, 128], BF16)).ap()
    ident_f = SG(nc.sbuf_tensor("ident_f", [128, 128], F32)).ap()
    btile = SG(nc.sbuf_tensor("btile", [128, D], F32)).ap()
    nega = SG(nc.sbuf_tensor("nega", [H, 2], F32)).ap()
    dtb = SG(nc.sbuf_tensor("dtb", [H, 1], F32)).ap()
    ones1f = SG(nc.sbuf_tensor("ones1f", [1, 128], F32)).ap()
    alpha_t = SG(nc.sbuf_tensor("alpha_t", [128, 4], F32)).ap()
    csts = SG(nc.sbuf_tensor("csts", [128, 4], F32)).ap()
    masks_b = SG12(nc.sbuf_tensor("masks_b", [128, 3 * 1024], BF16)).ap()
    oh_b = SG12(nc.sbuf_tensor("oh_b", [128, H * H], BF16)).ap()
    wtile = SG12(nc.sbuf_tensor("wtile", [128, D], F32)).ap()
    gH = SG12(nc.sbuf_tensor("gH", [H, T], F32)).ap()
    lnbH = SG12(nc.sbuf_tensor("lnbH", [H, T], F32)).ap()
    ones16 = SG12(nc.sbuf_tensor("ones16", [H, 128], F32)).ap()

    tc = []
    tc.append(p.dma("pool", ident_b[:, :], c_ident[:, :]))
    tc.append(p.dma("sp", ident_f[:, :], c_ident[:, :]))
    tc.append(p.dma("pool", masks_b[:, :], c_masks[:, :]))
    tc.append(p.dma("pool", oh_b[:, :], c_oh[:, :]))
    tc.append(p.dma("sp", wtile[:, :], c_wtile[:, :]))
    tc.append(p.dma("sp", btile[:, :], c_btile[:, :]))
    tc.append(p.dma("sp", dtb[:, :], dt_bias[:, :]))
    tk_a = p.dma("sp", nega[:, 1:2], a_log[:, :])
    tk_na = p.act(nega[:, 0:1], nega[:, 1:2], AF.Exp, deps=[tk_a])
    tc.append(p.ts(nega[:, 0:1], nega[:, 0:1], -1.0, None, op.mult, deps=[tk_na]))
    p.mset("dve", ones16[:, :], 1.0)
    tc.append(p.mset("dve", ones1f[:, :], 1.0))
    p.mset("dve", csts[:, 0:1], L2EPS, fence=False)
    tc.append(p.mset("dve", csts[:, 1:2], EPS))
    nc.const_aps.register(F32, L2EPS, csts[:, 0:1])
    nc.const_aps.register(F32, EPS, csts[:, 1:2])
    TC = tuple(tc)

    es1 = ExitStack()
    S1 = es1.enter_context
    xf = S1(nc.sbuf_tensor("xf", [128, 16 * T1], BF16)).ap()
    x32 = S1(nc.sbuf_tensor("x32", [128, 2 * D], F32)).ap()
    xb16 = S1(nc.sbuf_tensor("xb16", [128, 2 * D], BF16)).ap()
    wqb = S1(nc.sbuf_tensor("wqb", [128, 2 * 2048], BF16)).ap()
    wabb = S1(nc.sbuf_tensor("wabb", [128, 16 * 512], BF16)).ap()
    wabsm = S1(nc.sbuf_tensor("wabsm", [128, 16 * 32], BF16)).ap()
    qpre = S1(nc.sbuf_tensor("qpre", [128, T1 + 4], BF16)).ap()
    tails = S1(nc.sbuf_tensor("tails", [128, 3 * 48], BF16)).ap()
    sq_out = S1(nc.sbuf_tensor("sq_out", [128, 2 * T1], BF16)).ap()
    sgate = S1(nc.sbuf_tensor("sgate", [128, 2 * 512], BF16)).ap()
    sab = S1(nc.sbuf_tensor("sab", [128, 32 * (T1 // 128)], F32)).ap()
    wcv = S1(nc.sbuf_tensor("wcv", [128, 192], F32)).ap()
    diag4 = S1(nc.sbuf_tensor("diag4", [128, 4 * 128], BF16)).ap()
    st1 = S1(nc.sbuf_tensor("st1", [H, 6 * 128], F32)).ap()
    ps_q = S1(nc.psum_tensor("ps_q", [128, 1024], F32)).ap()
    ps_ab = S1(nc.psum_tensor("ps_ab", [128, 2048], F32)).ap()
    ps_s1 = S1(nc.psum_tensor("ps_s1", [128, 512], F32)).ap()

    tk_wcv = p.dma("pool", wcv[:, :].rearrange("p (fb j) -> p fb j", j=4),
                   conv_w[:, :].rearrange("(fb p) j -> p fb j", p=128))
    tk_t0 = p.mset("dve", tails[:, :], 0.0)
    tail_tok = {fb: tk_t0 for fb in range(48)}

    NTB1 = T1 // 128
    SEGW = min(512, T1)
    NSEG = T1 // SEGW
    GSZ = min(4, NTB1)

    def softplus(out_ap, in_ap, tmp_ap, deps, scale=1.0):
        p.act(tmp_ap, in_ap, AF.Abs, scale=scale, deps=deps, fence=False)
        p.act(tmp_ap, tmp_ap, AF.Exp, scale=-1.0, fence=False)
        p.act(tmp_ap, tmp_ap, AF.Ln, bias=1.0, fence=False)
        t1 = p.act(out_ap, in_ap, AF.Relu, scale=scale, deps=deps)
        return p.tt(out_ap, out_ap, tmp_ap, op.add, deps=[t1])

    for ti in range(NT1):
        t0 = ti * T1
        tk_xtr = []
        tk_cst_s = {}
        tk_trm_s = {}
        for tb in range(NTB1):
            sl = tb % 2
            tk_x = p.dma("sp", x32[:, sl * D:(sl + 1) * D],
                         x_in[t0 + tb * 128: t0 + (tb + 1) * 128, :],
                         deps=[tk_cst_s.get(sl)])
            tk_cst = p.cp("act", xb16[:, sl * D:(sl + 1) * D],
                          x32[:, sl * D:(sl + 1) * D],
                          deps=[tk_x, tk_trm_s.get(sl)])
            tk_cst_s[sl] = tk_cst
            tkm = None
            for g4 in range(4):
                prev = tk_xtr[-1] if tk_xtr else None
                for j in range(4):
                    dblk = g4 * 4 + j
                    tkm = p.mm(ps_s1[:, j * 128:(j + 1) * 128],
                               xb16[:, sl * D + dblk * 128: sl * D + (dblk + 1) * 128],
                               ident_b[:, :], True, True,
                               deps=[tk_cst, prev] + list(TC), fence=(j == 3))
                tk_xtr.append(p.cp(
                    "dve",
                    xf[:, :].rearrange("p (b tt) -> p b tt", b=16)
                    [:, g4 * 4:(g4 + 1) * 4, tb * 128:(tb + 1) * 128],
                    ps_s1[:, :].rearrange("p (b tt) -> p b tt", b=4)[:, :, :],
                    deps=[tkm]))
            tk_trm_s[sl] = tkm
        tk_xf = p.nopf("dve", deps=tk_xtr)
        p.barrier()

        tk_wq = {}
        tk_fb_done = {}
        tk_sq = {}

        def wq_dma(fb, deps):
            sl = fb % 2
            return p.dma("pool",
                         wqb[:, sl * 2048:(sl + 1) * 2048].rearrange(
                             "p (kt n) -> p kt n", n=128),
                         w_qkv[:, fb * 128:(fb + 1) * 128].rearrange(
                             "(kt p) n -> p kt n", p=128),
                         deps=deps)
        tk_wq[0] = wq_dma(0, [tk_wcv])
        tk_wq[1] = wq_dma(1, [tk_wcv])
        for fb in range(48):
            sl = fb % 2
            tk_ev = []
            for seg in range(NSEG):
                pslot = (seg % 2) * 512
                tkm = None
                for kt in range(16):
                    tkm = p.mm(ps_q[:, pslot:pslot + SEGW],
                               wqb[:, sl * 2048 + kt * 128: sl * 2048 + (kt + 1) * 128],
                               xf[:, kt * T1 + seg * SEGW: kt * T1 + (seg + 1) * SEGW],
                               kt == 0, kt == 15,
                               deps=[tk_wq[fb], tk_xf, tk_fb_done.get(fb - 1),
                                     tk_ev[seg - 2] if seg >= 2 else None],
                               fence=(kt == 15))
                tk_ev.append(p.act(qpre[:, 3 + seg * SEGW: 3 + (seg + 1) * SEGW],
                                   ps_q[:, pslot:pslot + SEGW], AF.Copy, deps=[tkm]))
            if fb + 2 < 48:
                tk_wq[fb + 2] = wq_dma(fb + 2, [tkm, tk_wcv])
            tk_tl = p.cp("dve", qpre[:, 0:3], tails[:, fb * 3:(fb + 1) * 3],
                         deps=[tail_tok[fb], tk_fb_done.get(fb - 1)])
            tail_tok[fb] = p.cp("dve", tails[:, fb * 3:(fb + 1) * 3],
                                qpre[:, T1: T1 + 3], deps=[tk_ev[-1], tk_tl])
            tkd = None
            for j in range(4):
                tkd = p.ts(diag4[:, j * 128:(j + 1) * 128], ident_b[:, :],
                           wcv[:, fb * 4 + j: fb * 4 + j + 1], None, op.mult,
                           deps=[tk_wcv, tk_fb_done.get(fb - 1)], fence=(j == 3))
            tk_cv = []
            for seg in range(NSEG):
                pslot = (seg % 2) * 512
                tkm = None
                for j in range(4):
                    tkm = p.mm(ps_q[:, pslot:pslot + SEGW],
                               diag4[:, j * 128:(j + 1) * 128],
                               qpre[:, j + seg * SEGW: j + seg * SEGW + SEGW],
                               j == 0, j == 3,
                               deps=[tkd, tk_ev[-1], tk_tl,
                                     tk_cv[seg - 2] if seg >= 2 else None],
                               fence=(j == 3))
                tk_cv.append(p.act(sq_out[:, sl * T1 + seg * SEGW: sl * T1 + (seg + 1) * SEGW],
                                   ps_q[:, pslot:pslot + SEGW], AF.Silu,
                                   deps=[tkm, tk_sq.get(fb - 2)]))
            tk_fb_done[fb] = tk_cv[-1]
            if fb < 16:
                blk = fb * 3 + 1
            elif fb < 32:
                blk = (fb - 16) * 3
            else:
                blk = (fb - 32) * 3 + 2
            c0 = t0 // 128
            tk_sq[fb] = p.dma("pool",
                              qkc_s[c0:c0 + NTB1, blk, :, :].rearrange(
                                  "cc pp tt -> pp cc tt"),
                              sq_out[:, sl * T1:(sl + 1) * T1].rearrange(
                                  "pp (cc tt) -> pp cc tt", tt=128),
                              deps=tk_cv)
        p.barrier()

        tk_wsm = [p.dma("pool", wabsm[:, kt * 32:(kt + 1) * 32],
                        w_ab[kt * 128:(kt + 1) * 128, 0:32]) for kt in range(16)]
        tk_absm = []
        for tb in range(NTB1):
            tkm = None
            for kt in range(16):
                tkm = p.mm(ps_s1[:, 0:32],
                           xf[:, kt * T1 + tb * 128: kt * T1 + (tb + 1) * 128],
                           wabsm[:, kt * 32:(kt + 1) * 32],
                           kt == 0, kt == 15,
                           deps=tk_wsm + [tk_xf, tk_absm[tb - 1] if tb >= 1 else None],
                           fence=(kt == 15))
            tk_absm.append(p.cp("dve", sab[:, tb * 32:(tb + 1) * 32],
                                ps_s1[:, 0:32], deps=[tkm]))
        p.barrier()

        tk_gdma = {}
        tk_gmm = {}
        for nch in range(4):
            ncol0 = 32 + nch * 512
            tk_wab = p.dma("pool", wabb[:, :].rearrange("p (kt n) -> p kt n", n=512),
                           w_ab[:, ncol0:ncol0 + 512].rearrange(
                               "(kt p) n -> p kt n", p=128),
                           deps=[tk_gmm.get(nch - 1)])
            for tbg in range(NTB1 // GSZ):
                tk_mm4 = []
                for tb4 in range(GSZ):
                    tb = tbg * GSZ + tb4
                    prevkey = (nch, tbg - 1, tb4) if tbg > 0 else (nch - 1, NTB1 // GSZ - 1, tb4)
                    tkm = None
                    for kt in range(16):
                        tkm = p.mm(ps_ab[:, tb4 * 512:(tb4 + 1) * 512],
                                   xf[:, kt * T1 + tb * 128: kt * T1 + (tb + 1) * 128],
                                   wabb[:, kt * 512:(kt + 1) * 512],
                                   kt == 0, kt == 15,
                                   deps=[tk_wab, tk_xf, tk_gdma.get(prevkey)],
                                   fence=(kt == 15))
                    tk_mm4.append(tkm)
                tk_sd = []
                for tb4 in range(GSZ):
                    tb = tbg * GSZ + tb4
                    sl = tb4 % 2
                    tk_si = p.act(sgate[:, sl * 512:(sl + 1) * 512],
                                  ps_ab[:, tb4 * 512:(tb4 + 1) * 512], AF.Silu,
                                  deps=[tk_mm4[tb4],
                                        tk_sd[tb4 - 2] if tb4 >= 2 else None])
                    tk_sd.append(p.dma("sp",
                                       sg_s[t0 + tb * 128: t0 + (tb + 1) * 128,
                                            ncol0 - 32: ncol0 - 32 + 512],
                                       sgate[:, sl * 512:(sl + 1) * 512],
                                       deps=[tk_si]))
                    tk_gdma[(nch, tbg, tb4)] = tk_sd[-1]
                tk_gmm[nch] = tk_mm4[-1]
        p.barrier()

        aT = st1[:, 0:128]
        bbT = st1[:, 128:256]
        u_ = st1[:, 256:384]
        tmp_ = st1[:, 384:512]
        sp_ = st1[:, 512:640]
        tk_bg = None
        for tb in range(NTB1):
            tka1 = p.tr(ps_s1[0:16, 0:128], sab[:, tb * 32: tb * 32 + 16],
                        ident_f[:, :], deps=[tk_absm[tb], tk_bg])
            tka2 = p.tr(ps_s1[0:16, 128:256], sab[:, tb * 32 + 16: tb * 32 + 32],
                        ident_f[:, :], deps=[tka1])
            tc1 = p.cp("dve", aT, ps_s1[0:16, 0:128], deps=[tka1])
            tc2 = p.cp("dve", bbT, ps_s1[0:16, 128:256], deps=[tka2])
            tk_u = p.ts(u_, aT, dtb[:, 0:1], None, op.add, deps=[tc1])
            tsp = softplus(sp_, u_, tmp_, deps=[tk_u])
            tk_g = p.ts(gH[:, t0 + tb * 128: t0 + (tb + 1) * 128], sp_,
                        nega[:, 0:1], None, op.mult, deps=[tsp])
            tsp2 = softplus(sp_, bbT, tmp_, deps=[tc2, tk_g], scale=-1.0)
            tk_bg = p.ts(lnbH[:, t0 + tb * 128: t0 + (tb + 1) * 128], sp_,
                         -1.0, LN2, op.mult, op.add, deps=[tsp2])
        p.barrier()

    if dbg:
        p.dma("sp", gH_d[:, :], gH[:, :])
        p.dma("sp", lnbH_d[:, :], lnbH[:, :])
    p.barrier()
    es1.close()

    esf = ExitStack()
    SF = esf.enter_context
    fw = SF(nc.sbuf_tensor("fw", [128, 2 * D], F32)).ap()
    facc = SF(nc.sbuf_tensor("facc", [128, 16], F32)).ap()
    ftmp = SF(nc.sbuf_tensor("ftmp", [128, D], F32)).ap()
    ftrn = SF(nc.sbuf_tensor("ftrn", [128, D], BF16)).ap()
    ftrn2 = SF(nc.sbuf_tensor("ftrn2", [128, D], BF16)).ap()
    fsm = SF(nc.sbuf_tensor("fsm", [1, 16], F32)).ap()
    fcol = SF(nc.sbuf_tensor("fcol", [128, 1], F32)).ap()
    ps_f = SF(nc.psum_tensor("ps_f", [128, 512], F32)).ap()

    tk_blk = []
    for rb in range(16):
        sl = rb % 2
        tk_l = p.dma("sp", fw[:, sl * D:(sl + 1) * D],
                     ffn_w[rb * 128:(rb + 1) * 128, :],
                     deps=[tk_blk[rb - 2] if rb >= 2 else None])
        tk_blk.append(p.red(facc[:, rb:rb + 1], fw[:, sl * D:(sl + 1) * D],
                            op.add, absv=True, deps=[tk_l]))
    tk_r = p.red(fcol[:, 0:1], facc[:, :], op.add, deps=tk_blk)
    tk_t = p.tr(ps_f[0:1, 0:128], fcol[:, 0:1], ident_f[:, :], deps=[tk_r])
    tk_r2 = p.red(fsm[0:1, 0:1], ps_f[0:1, 0:128], op.add, deps=[tk_t])
    p.ts(fsm[0:1, 1:2], fsm[0:1, 0:1], 1.0 / (D * D), EPS, op.mult, op.max,
         deps=[tk_r2], fence=False)
    p.raw("dve", lambda v: v.reciprocal(fsm[0:1, 2:3], fsm[0:1, 1:2]), fence=False)
    tk_al = p.ts(fsm[0:1, 3:4], fsm[0:1, 1:2], 1.0 / 127.0, None, op.mult)
    if dbg:
        p.dma("sp", alpha_d[:, :], fsm[0:1, 0:4], deps=[tk_al])
    tk_bc = p.mm(ps_f[:, 128:132], ones1f[:, :], fsm[0:1, 0:4], True, True,
                 deps=[tk_al, tk_t] + list(TC), fence=True)
    tk_at = p.cp("dve", alpha_t[:, 0:4], ps_f[:, 128:132], deps=[tk_bc])

    tk_pdma = None
    tk_rd = {}
    for rb in range(16):
        sl = rb % 2
        tk_l = p.dma("sp", fw[:, sl * D:(sl + 1) * D],
                     ffn_w[rb * 128:(rb + 1) * 128, :],
                     deps=[tk_at, tk_rd.get(rb - 2)])
        tk1 = p.act(ftmp[:, :], fw[:, sl * D:(sl + 1) * D], AF.Copy,
                    bias=MAGIC, scale=alpha_t[:, 2:3],
                    deps=[tk_l, tk_at, tk_rd.get(("ts", rb - 1))])
        tk_rd[rb] = tk1
        tk2 = p.act(ftmp[:, :], ftmp[:, :], AF.Copy, bias=-MAGIC, deps=[tk1])
        tk3 = p.ts(ftrn[:, :], ftmp[:, :], 1.0, -1.0, op.min, op.max,
                   deps=[tk2, tk_rd.get(("tr", rb - 1))])
        tk_rd[("ts", rb)] = tk3
        for ib in range(16):
            tkt = p.mm(ps_f[:, (ib % 4) * 128:(ib % 4 + 1) * 128],
                       ftrn[:, ib * 128:(ib + 1) * 128], ident_b[:, :], True, True,
                       deps=[tk3, tk_pdma], fence=True)
            tke = p.cp("dve", ftrn2[:, ib * 128:(ib + 1) * 128],
                       ps_f[:, (ib % 4) * 128:(ib % 4 + 1) * 128], deps=[tkt])
            tk_pdma = p.dma("pool",
                            tern_s[ib * 128:(ib + 1) * 128, rb * 128:(rb + 1) * 128],
                            ftrn2[:, ib * 128:(ib + 1) * 128], deps=[tke])
            tk_rd[("tr", rb)] = tkt
    p.barrier()
    esf.close()

    _stage2(nc, p, es, T, NCH, qkc_s, sg_s, of_s, gH, lnbH, ident_b, ident_f,
            masks_b, oh_b, wtile, c_ind, ones16, ones1f)
    p.barrier()
    es12.close()

    _stage3(nc, p, es, T, T3, NT3, x_in, w_o, of_s, tern_s, y_out,
            ident_b, ident_f, btile, alpha_t)

    p.replay()
    es.close()
    return nc


def _stage2(nc, p, es, T, NCH, qkc_s, sg_s, of_s, gH, lnbH, ident_b, ident_f,
            masks_b, oh_b, wtile, c_ind, ones16, ones1f):
    es2 = ExitStack()
    SG = es2.enter_context
    qc = SG(nc.sbuf_tensor("qc", [128, 2 * 6144], BF16)).ap()
    sgc = SG(nc.sbuf_tensor("sgc", [128, 2 * D], BF16)).ap()
    S_sb = SG(nc.sbuf_tensor("S_sb", [128, H * 128], F32)).ap()
    S_bf = SG(nc.sbuf_tensor("S_bf", [128, H * 128], BF16)).ap()
    sqb = SG(nc.sbuf_tensor("sqb", [128, 2 * D], BF16)).ap()
    indb = SG(nc.sbuf_tensor("indb", [51, 3 * D], BF16)).ap()
    lA = SG(nc.sbuf_tensor("lA", [51, 128], BF16)).ap()
    lST = SG(nc.sbuf_tensor("lST", [51, 128], BF16)).ap()
    mst = SG(nc.sbuf_tensor("mst", [H, 8 * 128], F32)).ap()
    mpc = SG(nc.sbuf_tensor("mpc", [H, 9 * 128], BF16)).ap()
    lnssq = SG(nc.sbuf_tensor("lnssq", [H, 256], F32)).ap()
    cols = SG(nc.sbuf_tensor("cols", [128, 64], F32)).ap()
    expd = SG(nc.sbuf_tensor("expd", [128, 1024], F32)).ap()
    ktok = SG(nc.sbuf_tensor("ktok", [128, 1024], BF16)).ap()
    kdec = SG(nc.sbuf_tensor("kdec", [128, 2048], BF16)).ap()
    kff = SG(nc.sbuf_tensor("kff", [128, 1024], BF16)).ap()
    ntok = SG(nc.sbuf_tensor("ntok", [128, 2 * 1024], BF16)).ap()
    nbar = SG(nc.sbuf_tensor("nbar", [128, 2 * 1024], BF16)).ap()
    wch = SG(nc.sbuf_tensor("wch", [128, 2 * 1024], BF16)).ap()
    wbar = SG(nc.sbuf_tensor("wbar", [128, 2048], BF16)).ap()
    ub = SG(nc.sbuf_tensor("ub", [128, 2048], BF16)).ap()
    ost = SG(nc.sbuf_tensor("ost", [128, 2048], F32)).ap()
    ogt = SG(nc.sbuf_tensor("ogt", [128, 2048], F32)).ap()
    onb = SG(nc.sbuf_tensor("onb", [128, 2048], BF16)).ap()
    ofst = SG(nc.sbuf_tensor("ofst", [128, 2048], BF16)).ap()
    rms = SG(nc.sbuf_tensor("rms", [128, 48], F32)).ap()
    qst = SG(nc.sbuf_tensor("qst", [128, 1024], F32)).ap()
    grow = SG(nc.sbuf_tensor("grow", [1, 16], F32)).ap()
    gcb = SG(nc.sbuf_tensor("gcb", [128, 16], F32)).ap()
    PA = SG(nc.psum_tensor("PA", [128, 1024], F32)).ap()
    PB = SG(nc.psum_tensor("PB", [128, 1024], F32)).ap()
    PC = SG(nc.psum_tensor("PC", [128, 1024], F32)).ap()
    PD = SG(nc.psum_tensor("PD", [128, 1024], F32)).ap()

    p.mset("dve", S_sb[:, :], 0.0, fence=False)
    p.mset("dve", S_bf[:, :], 0.0, fence=False)
    p.mset("dve", lA[0:51, :], 1.0, fence=False)
    tk_init = p.mset("dve", lST[0:51, :], 1.0)
    tk_ind = p.dma("pool", indb[0:48, :], c_ind[:, :])

    tk_qc = {0: p.dma("sp", qc[:, 0:6144].rearrange("pp (b t) -> pp b t", t=128),
                      qkc_s[0, :, :, :].rearrange("b pp t -> pp b t"))}
    tk_sg = {0: p.dma("sp", sgc[:, 0:D], sg_s[0:128, :])}

    tk_Sw = {h: tk_init for h in range(H)}
    tk_used = tk_init
    MSL = dict(gc=0, m1=1, m2=2, m3=3, m4=4, tmp=5, exG=6)

    def msl(nm):
        i = MSL[nm]
        return mst[:, i * 128:(i + 1) * 128]

    for c in range(NCH):
        par = c % 2
        qcs = qc[:, par * 6144:(par + 1) * 6144]
        sgs = sgc[:, par * D:(par + 1) * D]
        if c + 1 < NCH:
            tk_qc[c + 1] = p.dma("sp",
                                 qc[:, (1 - par) * 6144:(2 - par) * 6144].rearrange(
                                     "pp (b t) -> pp b t", t=128),
                                 qkc_s[c + 1, :, :, :].rearrange("b pp t -> pp b t"),
                                 deps=[tk_used])
            tk_sg[c + 1] = p.dma("sp", sgc[:, (1 - par) * D:(2 - par) * D],
                                 sg_s[(c + 1) * 128:(c + 2) * 128, :],
                                 deps=[tk_used])

        tk_gc = p.raw("dve", lambda v, d=msl("gc"), o=ones16[:, :],
                      g_=gH[:, c * 128:(c + 1) * 128]:
                      v.tensor_tensor_scan(d, o, g_, 0.0, op.mult, op.add),
                      deps=[tk_init, tk_used])
        tk_k2 = p.act(sqb[:, :].rearrange("p (h kq t) -> p h kq t", h=H, kq=2)[:, :, 0, :],
                      qcs.rearrange("p (h z t) -> p h z t", h=H, z=3)[:, :, 0, :],
                      AF.Square, deps=[tk_qc[c], tk_used])
        tk_q2 = p.act(sqb[:, :].rearrange("p (h kq t) -> p h kq t", h=H, kq=2)[:, :, 1, :],
                      qcs.rearrange("p (h z t) -> p h z t", h=H, z=3)[:, :, 1, :],
                      AF.Square, deps=[tk_qc[c]])
        tkm = None
        for h in range(H):
            tkm = p.mm(PD[0:16, 0:256], oh_b[:, h * 16:(h + 1) * 16],
                       sqb[:, h * 256:(h + 1) * 256], h == 0, h == 15,
                       deps=[tk_k2, tk_q2, tk_used], fence=(h == 15))
        tk_ln = p.act(lnssq[:, :], PD[0:16, 0:256], AF.Ln, bias=L2EPS, deps=[tkm])
        p.stt(msl("m1"), lnssq[:, 0:128], -0.5, msl("gc"), op.mult, op.add,
              deps=[tk_ln, tk_gc], fence=False)
        p.stt(msl("tmp"), lnssq[:, 0:128], -0.5, lnbH[:, c * 128:(c + 1) * 128],
              op.mult, op.add, deps=[tk_ln], fence=False)
        p.tt(msl("m2"), msl("tmp"), msl("gc"), op.subtract, fence=False)
        p.stt(msl("tmp"), lnssq[:, 128:256], -0.5, msl("gc"), op.mult, op.add,
              fence=False)
        p.ts(msl("m3"), msl("tmp"), LN_HALF_DK, None, op.add, fence=False)
        tk_m = p.ts(msl("m4"), msl("m2"), msl("gc")[:, 127:128], None, op.add)
        tk_pc = []
        for mi, nm in enumerate(("m1", "m2", "m3")):
            b0 = mi * 3
            p.cp("dve", mpc[:, b0 * 128:(b0 + 1) * 128], msl(nm), deps=[tk_m],
                 fence=False)
            p.tt(msl("tmp"), msl(nm), mpc[:, b0 * 128:(b0 + 1) * 128], op.subtract,
                 fence=False)
            p.cp("dve", mpc[:, (b0 + 1) * 128:(b0 + 2) * 128], msl("tmp"), fence=False)
            p.tt(msl("tmp"), msl("tmp"), mpc[:, (b0 + 1) * 128:(b0 + 2) * 128],
                 op.subtract, fence=False)
            tk_pc.append(p.cp("dve", mpc[:, (b0 + 2) * 128:(b0 + 3) * 128], msl("tmp")))
        tk_pc = p.nopf("dve", deps=tk_pc)
        tk_asm = []
        for j in range(3):
            tk_asm.append(p.dma("sp", lA[16 * j:16 * (j + 1), :],
                                mpc[:, j * 128:(j + 1) * 128],
                                deps=[tk_pc, tk_used]))
            tk_asm.append(p.dma("sp", lST[16 * j:16 * (j + 1), :],
                                mpc[:, (3 + j) * 128:(4 + j) * 128],
                                deps=[tk_pc, tk_used]))
            for orient, mb in ((0, 3), (1, 0), (2, 6)):
                tk_asm.append(p.dma("sp",
                                    indb[48 + j:49 + j, orient * D:orient * D + D],
                                    mpc[:, (mb + j) * 128:(mb + j + 1) * 128],
                                    deps=[tk_pc, tk_used, tk_ind]))
        tk_tr1 = p.tr(PD[:, 256:272], msl("m1"), ident_f[0:16, 0:16], deps=[tk_m, tkm])
        tk_tr2 = p.tr(PD[:, 272:288], msl("m4"), ident_f[0:16, 0:16], deps=[tk_m])
        tk_tr3 = p.tr(PD[:, 288:304], msl("m3"), ident_f[0:16, 0:16], deps=[tk_m])
        tk_cols = p.act(cols[:, 0:48], PD[:, 256:304], AF.Exp,
                        deps=[tk_tr1, tk_tr2, tk_tr3])
        tk_xg = p.act(msl("exG")[:, 0:1], msl("gc")[:, 127:128], AF.Exp, deps=[tk_gc])
        tk_eg = p.tr(PD[0:1, 304:320], msl("exG")[:, 0:1], ident_f[0:16, 0:16],
                     deps=[tk_xg, tk_cols])
        tk_egc = p.cp("dve", grow[0:1, 0:16], PD[0:1, 304:320], deps=[tk_eg])
        tk_gcb = p.mm(PD[:, 320:336], ones1f[:, :], grow[0:1, 0:16], True, True,
                      deps=[tk_egc], fence=True)
        tk_gcb = p.cp("dve", gcb[:, 0:16], PD[:, 320:336], deps=[tk_gcb])

        tk_of_dma = []
        tk_Send = []
        for g in range(2):
            h0 = g * 8

            def kqv(hh):
                b = ((h0 + hh) * 3) * 128
                return (qcs[:, b + 128:b + 256], qcs[:, b:b + 128],
                        qcs[:, b + 256:b + 384])

            tkm = None
            for hh in range(8):
                q_, k_, v_ = kqv(hh)
                tkm = p.mm(PA[:, hh * 128:(hh + 1) * 128], k_, ident_b[:, :],
                           True, True,
                           deps=[tk_qc[c], tk_cols, tk_gcb, tk_Sw[(h0 - 1) % H]],
                           fence=(hh == 7))
            tk_ksc = []
            for hh in range(8):
                h = h0 + hh
                tk_ksc.append(p.ts(ktok[:, hh * 128:(hh + 1) * 128],
                                   PA[:, hh * 128:(hh + 1) * 128],
                                   cols[:, h:h + 1], -1.0, op.mult, op.mult,
                                   deps=[tkm]))
                tk_ksc.append(p.ts(kdec[:, h * 128:(h + 1) * 128],
                                   PA[:, hh * 128:(hh + 1) * 128],
                                   cols[:, 16 + h:16 + h + 1], None, op.mult,
                                   deps=[tkm]))
            tk_ksc = p.nopf("dve", deps=tk_ksc)
            tkm = None
            for hh in range(8):
                tkm = p.mm(PB[:, hh * 128:(hh + 1) * 128],
                           ktok[:, hh * 128:(hh + 1) * 128], ident_b[:, :],
                           True, True, deps=[tk_ksc], fence=(hh == 7))
            tk_kff = p.cp("dve", kff[:, :], PB[:, :], deps=[tkm])
            tkm = None
            for hh in range(8):
                q_, k_, v_ = kqv(hh)
                tkm = p.mm(PC[:, hh * 128:(hh + 1) * 128], v_, ident_b[:, :],
                           hh % 4 == 0, False, deps=[tk_qc[c], tk_kff],
                           fence=False)
            for hh in range(8):
                h = h0 + hh
                tkm = p.mm(PC[:, hh * 128:(hh + 1) * 128],
                           kff[:, hh * 128:(hh + 1) * 128],
                           S_bf[:, h * 128:(h + 1) * 128], False, hh % 4 == 3,
                           deps=[tk_kff, tk_Sw[h]], fence=(hh == 7))
            tk_w0 = p.cp("dve", wch[:, 0:1024], PC[:, :], deps=[tkm])
            tkm = None
            for hh in range(8):
                q_, k_, v_ = kqv(hh)
                dst = PA if hh < 4 else PB
                off = (hh % 4) * 256
                b = ((h0 + hh) * 3) * 128
                tkm = p.mm(dst[:, off:off + 256], k_, qcs[:, b:b + 256],
                           True, True, deps=[tk_kff, tk_w0, tk_qc[c]],
                           fence=(hh == 7))
            tk_gram = tkm

            def diff_build(orient, mask_i, out_sb, sign, kq, deps):
                lhs = lA if orient == 0 else lST
                tkk = None
                for half in range(2):
                    tkk = p.mm(PD[:, half * 512:(half + 1) * 512], lhs[:, :],
                               indb[:, orient * D + g * 1024 + half * 512:
                                    orient * D + g * 1024 + (half + 1) * 512],
                               True, False,
                               deps=list(deps) + tk_asm + [tk_ind], fence=False)
                    tkk = p.mm(PD[:, half * 512:(half + 1) * 512], ident_b[:, :],
                               masks_b[:, mask_i * 1024 + half * 512:
                                       mask_i * 1024 + (half + 1) * 512],
                               False, True, deps=[], fence=(half == 1))
                tke = p.act(expd[:, :], PD[:, :], AF.Exp, deps=[tkk] + list(deps))
                tks = []
                for half in range(2):
                    dst = PA if half == 0 else PB
                    tks.append(p.stt(
                        out_sb.rearrange("p (hh s) -> p hh s", hh=8)
                        [:, half * 4:(half + 1) * 4, :],
                        expd[:, :].rearrange("p (hh s) -> p hh s", hh=8)
                        [:, half * 4:(half + 1) * 4, :],
                        float(sign),
                        dst[:, :].rearrange("p (hh kq s) -> p hh kq s", hh=4, kq=2)
                        [:, :, kq, :],
                        op.mult, op.mult, deps=[tke, tk_gram]))
                return p.nopf("dve", deps=tks), tke

            tk_n0, tke0 = diff_build(0, 0, ntok[:, 0:1024], -1.0, 0, [tk_gram])
            tk_nb0, tke1 = diff_build(1, 1, nbar[:, 0:1024], -1.0, 0, [tke0, tk_n0])
            tk_wb, tke2 = diff_build(2, 2, wbar[:, h0 * 128:(h0 + 8) * 128],
                                     1.0, 1, [tke1, tk_nb0])

            tk_wcur, tk_ncur, tk_bcur = tk_w0, tk_n0, tk_nb0
            cur = 0
            for lev in range(7):
                tkm = None
                for hh in range(8):
                    tkm = p.mm(PC[:, hh * 128:(hh + 1) * 128],
                               nbar[:, cur * 1024 + hh * 128: cur * 1024 + (hh + 1) * 128],
                               wch[:, (lev % 2) * 1024 + hh * 128:
                                   (lev % 2) * 1024 + (hh + 1) * 128],
                               True, True, deps=[tk_wcur, tk_bcur, tk_wb],
                               fence=(hh == 7))
                tk_wcur = p.tt(wch[:, ((lev + 1) % 2) * 1024:
                                   ((lev + 1) % 2) * 1024 + 1024],
                               wch[:, (lev % 2) * 1024:(lev % 2) * 1024 + 1024],
                               PC[:, :], op.add, deps=[tkm])
                if lev < 6:
                    tks = None
                    for hh in range(8):
                        tks = p.mm(PA[:, hh * 128:(hh + 1) * 128],
                                   nbar[:, cur * 1024 + hh * 128:
                                        cur * 1024 + (hh + 1) * 128],
                                   ntok[:, cur * 1024 + hh * 128:
                                        cur * 1024 + (hh + 1) * 128],
                                   True, True,
                                   deps=[tk_ncur, tk_bcur, tk_wb, tk_n0],
                                   fence=False)
                    for hh in range(8):
                        tks = p.mm(PB[:, hh * 128:(hh + 1) * 128],
                                   ntok[:, cur * 1024 + hh * 128:
                                        cur * 1024 + (hh + 1) * 128],
                                   nbar[:, cur * 1024 + hh * 128:
                                        cur * 1024 + (hh + 1) * 128],
                                   True, True, deps=[], fence=(hh == 7))
                    nxt = 1 - cur
                    tk_ncur = p.cp("dve", ntok[:, nxt * 1024:(nxt + 1) * 1024],
                                   PA[:, :], deps=[tks])
                    tk_bcur = p.cp("dve", nbar[:, nxt * 1024:(nxt + 1) * 1024],
                                   PB[:, :], deps=[tks])
                    cur = nxt
            tk_U = p.cp("dve", ub[:, g * 1024:(g + 1) * 1024],
                        wch[:, (7 % 2) * 1024:(7 % 2) * 1024 + 1024], deps=[tk_wcur])

            tkm = None
            for hh in range(8):
                q_, k_, v_ = kqv(hh)
                h = h0 + hh
                tkm = p.mm(PA[:, hh * 128:(hh + 1) * 128], q_,
                           S_bf[:, h * 128:(h + 1) * 128], True, True,
                           deps=[tk_U, tk_Sw[h], tk_ncur, tk_bcur], fence=False)
            for hh in range(8):
                h = h0 + hh
                tkm = p.mm(PB[:, hh * 128:(hh + 1) * 128],
                           wbar[:, h * 128:(h + 1) * 128],
                           ub[:, h * 128:(h + 1) * 128], True, True,
                           deps=[tk_U, tk_wb, tk_bcur], fence=(hh == 7))
            tk_o = tkm
            tk_qs = []
            for hh in range(8):
                h = h0 + hh
                tk_qs.append(p.ts(qst[:, hh * 128:(hh + 1) * 128],
                                  PA[:, hh * 128:(hh + 1) * 128],
                                  cols[:, 32 + h:32 + h + 1], None, op.mult,
                                  deps=[tk_o]))
            tk_ov = p.stt(ost[:, g * 1024:(g + 1) * 1024], qst[:, :], 1.0,
                          PB[:, :], op.mult, op.add, deps=tk_qs + [tk_o])
            tk_og = p.tt(ogt[:, g * 1024:(g + 1) * 1024],
                         ost[:, g * 1024:(g + 1) * 1024],
                         sgs[:, g * 1024:(g + 1) * 1024], op.mult,
                         deps=[tk_ov, tk_sg[c]])
            tk_s2 = p.act(ost[:, g * 1024:(g + 1) * 1024],
                          ogt[:, g * 1024:(g + 1) * 1024], AF.Square, deps=[tk_og])
            tk_red = p.red(rms[:, g * 8:(g + 1) * 8],
                           ost[:, :].rearrange("p (hh s) -> p hh s", hh=16)
                           [:, g * 8:(g + 1) * 8, :], op.add, deps=[tk_s2])
            p.act(rms[:, 16 + g * 8:16 + (g + 1) * 8], rms[:, g * 8:(g + 1) * 8],
                  AF.Ln, bias=EPS, scale=1.0 / 128.0, deps=[tk_red], fence=False)
            tk_rs = p.act(rms[:, 32 + g * 8:32 + (g + 1) * 8],
                          rms[:, 16 + g * 8:16 + (g + 1) * 8], AF.Exp, scale=-0.5)
            tk_on = []
            for hh in range(8):
                h = h0 + hh
                tk_on.append(p.stt(onb[:, h * 128:(h + 1) * 128],
                                   ogt[:, g * 1024 + hh * 128:
                                       g * 1024 + (hh + 1) * 128],
                                   rms[:, 32 + g * 8 + hh:32 + g * 8 + hh + 1],
                                   wtile[:, h * 128:(h + 1) * 128],
                                   op.mult, op.mult, deps=[tk_rs, tk_og]))
            tkm = None
            for hh in range(8):
                h = h0 + hh
                tkm = p.mm(PC[:, hh * 128:(hh + 1) * 128],
                           onb[:, h * 128:(h + 1) * 128], ident_b[:, :],
                           True, True, deps=[tk_on[hh], tk_U], fence=(hh == 7))
            tk_ofe = p.cp("dve", ofst[:, g * 1024:(g + 1) * 1024], PC[:, :],
                          deps=[tkm])
            tk_of_dma.append(p.dma("pool",
                                   of_s[c, h0:h0 + 8, :, :].rearrange(
                                       "hh pp tt -> pp hh tt"),
                                   ofst[:, g * 1024:(g + 1) * 1024].rearrange(
                                       "pp (hh tt) -> pp hh tt", tt=128),
                                   deps=[tk_ofe]))
            tkm = None
            for hh in range(8):
                h = h0 + hh
                tkm = p.mm(PA[:, hh * 128:(hh + 1) * 128],
                           kdec[:, h * 128:(h + 1) * 128],
                           ub[:, h * 128:(h + 1) * 128], True, True,
                           deps=[tk_U, tk_ksc] + tk_qs, fence=(hh == 7))
            for hh in range(8):
                h = h0 + hh
                tk_su = p.stt(S_sb[:, h * 128:(h + 1) * 128],
                              S_sb[:, h * 128:(h + 1) * 128],
                              gcb[:, h:h + 1], PA[:, hh * 128:(hh + 1) * 128],
                              op.mult, op.add, deps=[tkm, tk_gcb])
                tk_Sw[h] = p.cp("dve", S_bf[:, h * 128:(h + 1) * 128],
                                S_sb[:, h * 128:(h + 1) * 128], deps=[tk_su])
                tk_Send.append(tk_Sw[h])
        tk_used = p.nopf("dve", deps=tk_of_dma + tk_Send)
    es2.close()


def _stage3(nc, p, es, T, T3, NT3, x_in, w_o, of_s, tern_s, y_out,
            ident_b, ident_f, btile, alpha_t):
    SG = es.enter_context
    wo_sb = SG(nc.sbuf_tensor("wo_sb", [128, 16 * D], BF16)).ap()
    tern_sb = SG(nc.sbuf_tensor("tern_sb", [128, 16 * D], BF16)).ap()
    ofsb = SG(nc.sbuf_tensor("ofsb", [128, 2 * D], BF16)).ap()
    xin = SG(nc.sbuf_tensor("xin", [128, D], F32)).ap()
    xr = SG(nc.sbuf_tensor("xr", [128, D], F32)).ap()
    scr = SG(nc.sbuf_tensor("scr", [128, D], F32)).ap()
    yb = SG(nc.sbuf_tensor("yb", [128, D], F32)).ap()
    qx = SG(nc.sbuf_tensor("qx", [128, D], BF16)).ap()
    qxf = SG(nc.sbuf_tensor("qxf", [128, D], BF16)).ap()
    zb = SG(nc.sbuf_tensor("zb", [128, D], F32)).ap()
    outb = SG(nc.sbuf_tensor("outb", [128, D], F32)).ap()
    stats = SG(nc.sbuf_tensor("stats", [128, 16], F32)).ap()
    psA = SG(nc.psum_tensor("psA", [128, 2048], F32)).ap()
    psB = SG(nc.psum_tensor("psB", [128, 2048], F32)).ap()

    tk_wo = [p.dma("pool", wo_sb[:, kt * D:(kt + 1) * D],
                   w_o[kt * 128:(kt + 1) * 128, :]) for kt in range(16)]
    tk_tern = [p.dma("sp", tern_sb[:, kt * D:(kt + 1) * D],
                     tern_s[kt * 128:(kt + 1) * 128, :]) for kt in range(16)]
    tk_out = []
    prev = {}
    for ti in range(NT3):
        t0 = ti * T3
        c0 = t0 // 128
        ntb = T3 // 128
        tk_of = {0: p.dma("sp",
                          ofsb[:, 0:D].rearrange("pp (h tt) -> pp h tt", tt=128),
                          of_s[c0, :, :, :].rearrange("h pp tt -> pp h tt"),
                          deps=[prev.get(("tile_pe", ti - 1))])}
        for tb in range(ntb):
            sl = tb % 2
            if tb + 1 < ntb:
                tk_of[tb + 1] = p.dma(
                    "sp",
                    ofsb[:, ((tb + 1) % 2) * D:((tb + 1) % 2 + 1) * D].rearrange(
                        "pp (h tt) -> pp h tt", tt=128),
                    of_s[c0 + tb + 1, :, :, :].rearrange("h pp tt -> pp h tt"),
                    deps=[prev.get(("attnpe", (ti, tb - 1))),
                          prev.get(("tile_pe", ti - 1))])
            tk_x = p.dma("sp", xin[:, :],
                         x_in[t0 + tb * 128: t0 + (tb + 1) * 128, :],
                         deps=[prev.get(("s1", (ti, tb - 1)))])
            tkm = None
            for oc in range(4):
                for kt in range(16):
                    tkm = p.mm(psA[:, oc * 512:(oc + 1) * 512],
                               ofsb[:, sl * D + kt * 128: sl * D + (kt + 1) * 128],
                               wo_sb[:, kt * D + oc * 512: kt * D + (oc + 1) * 512],
                               kt == 0, kt == 15,
                               deps=[tk_of[tb], prev.get(("zr", (ti, tb - 1)))] + tk_wo,
                               fence=(oc == 3 and kt == 15))
            tk_attn = tkm
            prev[("attnpe", (ti, tb))] = tk_attn
            tk_s1 = p.stt(xr[:, :], xin[:, :], 1.0, psA[:, :],
                          op.mult, op.add, deps=[tk_attn, tk_x],
                          accum=stats[:, 0:1])
            prev[("s1", (ti, tb))] = tk_s1
            tk_s2 = p.act(scr[:, :], xr[:, :], AF.Square, deps=[tk_s1],
                          accum=stats[:, 1:2])
            p.ts(stats[:, 2:3], stats[:, 0:1], 1.0 / D, None, op.mult,
                 deps=[tk_s2], fence=False)
            p.tt(stats[:, 3:4], stats[:, 2:3], stats[:, 2:3], op.mult, fence=False)
            tk_var = p.stt(stats[:, 4:5], stats[:, 1:2], 1.0 / D, stats[:, 3:4],
                           op.mult, op.subtract)
            p.act(stats[:, 5:6], stats[:, 4:5], AF.Ln, bias=EPS, deps=[tk_var],
                  fence=False)
            tk_st = p.act(stats[:, 6:7], stats[:, 5:6], AF.Exp, scale=-0.5)
            tk_y = p.ts(yb[:, :], xr[:, :], stats[:, 2:3], stats[:, 6:7],
                        op.subtract, op.mult, deps=[tk_st])
            tk_am = p.red(stats[:, 7:8], yb[:, :], op.max, absv=True, deps=[tk_y])
            p.ts(stats[:, 8:9], stats[:, 7:8], EPS, None, op.max, deps=[tk_am],
                 fence=False)
            p.raw("dve", lambda v: v.reciprocal(stats[:, 9:10], stats[:, 8:9]),
                  fence=False)
            p.ts(stats[:, 10:11], stats[:, 9:10], 127.0, None, op.mult, fence=False)
            tk_so = p.tt(stats[:, 11:12], stats[:, 8:9], alpha_t[:, 3:4], op.mult)
            tk_q1 = p.act(scr[:, :], yb[:, :], AF.Copy, bias=MAGIC,
                          scale=stats[:, 10:11], deps=[tk_so])
            tk_q2 = p.act(qx[:, :], scr[:, :], AF.Copy, bias=-MAGIC, deps=[tk_q1])
            tk_tr = []
            for g4 in range(4):
                poff = 1024 + (g4 % 2) * 512
                tkm = None
                for j in range(4):
                    dblk = g4 * 4 + j
                    tkm = p.mm(psB[:, poff + j * 128: poff + (j + 1) * 128],
                               qx[:, dblk * 128:(dblk + 1) * 128], ident_b[:, :],
                               True, True,
                               deps=[tk_q2, tk_tr[g4 - 2] if g4 >= 2 else
                                     prev.get(("qxf", (ti, tb - 1)))],
                               fence=(j == 3))
                tk_tr.append(p.cp("dve", qxf[:, g4 * 512:(g4 + 1) * 512],
                                  psB[:, poff:poff + 512], deps=[tkm]))
            tk_qxf = p.nopf("dve", deps=tk_tr)
            prev[("qxf", (ti, tb))] = tk_qxf
            tk_f = []
            for oc in range(4):
                dst = psB if oc < 2 else psA
                off = (oc % 2) * 512
                tkm = None
                for kt in range(16):
                    tkm = p.mm(dst[:, off:off + 512],
                               qxf[:, kt * 128:(kt + 1) * 128],
                               tern_sb[:, kt * D + oc * 512: kt * D + (oc + 1) * 512],
                               kt == 0, kt == 15,
                               deps=[tk_qxf, tk_s1 if oc >= 2 else
                                     prev.get(("zr", (ti, tb - 1)))] + tk_tern,
                               fence=(kt == 15))
                tk_f.append(tkm)
            prev[("tile_pe", ti)] = tk_f[-1]
            tk_z = []
            for oc in range(4):
                dst = psB if oc < 2 else psA
                off = (oc % 2) * 512
                tk_z.append(p.stt(zb[:, oc * 512:(oc + 1) * 512],
                                  dst[:, off:off + 512], stats[:, 11:12],
                                  btile[:, oc * 512:(oc + 1) * 512],
                                  op.mult, op.add, deps=[tk_f[oc], tk_so]))
            tk_zr = p.tt(zb[:, :], zb[:, :], yb[:, :], op.add, deps=tk_z)
            prev[("zr", (ti, tb))] = tk_zr
            tk_c1 = p.act(scr[:, :], zb[:, :], AF.Copy, deps=[tk_zr],
                          accum=stats[:, 0:1])
            tk_c2 = p.act(scr[:, :], zb[:, :], AF.Square, deps=[tk_c1],
                          accum=stats[:, 1:2])
            p.ts(stats[:, 2:3], stats[:, 0:1], 1.0 / D, None, op.mult,
                 deps=[tk_c2], fence=False)
            p.tt(stats[:, 3:4], stats[:, 2:3], stats[:, 2:3], op.mult, fence=False)
            tk_v2 = p.stt(stats[:, 4:5], stats[:, 1:2], 1.0 / D, stats[:, 3:4],
                          op.mult, op.subtract)
            p.act(stats[:, 5:6], stats[:, 4:5], AF.Ln, bias=EPS, deps=[tk_v2],
                  fence=False)
            tk_st2 = p.act(stats[:, 6:7], stats[:, 5:6], AF.Exp, scale=-0.5)
            tk_o = p.ts(outb[:, :], zb[:, :], stats[:, 2:3],
                        stats[:, 6:7], op.subtract, op.mult,
                        deps=[tk_st2, tk_out[-1] if tk_out else None])
            prev[("ln1", (ti, tb))] = tk_o
            tk_out.append(p.dma("sp", y_out[t0 + tb * 128: t0 + (tb + 1) * 128, :],
                                outb[:, :], deps=[tk_o]))

    p.wait_self_dma("sp", tk_out[-1])


def make_aux():
    ident = np.eye(128, dtype=np.float32)
    mA = np.where(np.tril(np.ones((128, 128)), -1) > 0, 0.0, NEG).astype(np.float32)
    mNb = np.where(np.triu(np.ones((128, 128)), 1) > 0, 0.0, NEG).astype(np.float32)
    mW = np.where(np.triu(np.ones((128, 128)), 0) > 0, 0.0, NEG).astype(np.float32)
    masks = np.concatenate([np.tile(m, (1, 8)) for m in (mA, mNb, mW)], axis=1)
    ind = np.zeros((48, D), np.float32)
    for j in range(3):
        for h in range(H):
            ind[j * 16 + h, h * 128:(h + 1) * 128] = 1.0
    ind3 = np.tile(ind, (1, 3))
    oh = np.zeros((128, H * H), np.float32)
    for h in range(H):
        oh[:, h * 16 + h] = 1.0
    return ident, masks, ind3, oh


def make_inputs(xb, W_qkv, W_ab, conv_w, A_log, dt_bias, o_norm_w, W_o, fw, fb):
    ident, masks, ind3, oh = make_aux()
    wt = np.ascontiguousarray(
        np.broadcast_to(np.tile(np.asarray(o_norm_w, np.float32), H), (128, D)))
    bt = np.ascontiguousarray(
        np.broadcast_to(np.asarray(fb, np.float32), (128, D)))
    return dict(
        x=np.ascontiguousarray(xb, np.float32),
        w_qkv=np.ascontiguousarray(W_qkv, np.float32),
        w_ab=np.ascontiguousarray(W_ab, np.float32),
        conv_w=np.ascontiguousarray(conv_w, np.float32),
        a_log=np.asarray(A_log, np.float32).reshape(H, 1),
        dt_bias=np.asarray(dt_bias, np.float32).reshape(H, 1),
        w_o=np.ascontiguousarray(W_o, np.float32),
        ffn_w=np.ascontiguousarray(fw, np.float32),
        c_ident=ident, c_masks=masks, c_ind=ind3, c_oh=oh,
        c_wtile=wt, c_btile=bt,
    )


def kernel(L0_attn_hidden_states, L0_attn_W_qkv, L0_attn_W_ab, L0_attn_conv_w,
           L0_attn_A_log, L0_attn_dt_bias, L0_attn_o_norm_w, L0_attn_W_o,
           L0_ffn_weight, L0_ffn_bias):
    x = np.ascontiguousarray(L0_attn_hidden_states, dtype=np.float32)
    B, T, _ = x.shape
    nc = build_program(T=T)
    in_maps = [make_inputs(x[b], L0_attn_W_qkv, L0_attn_W_ab, L0_attn_conv_w,
                           L0_attn_A_log, L0_attn_dt_bias, L0_attn_o_norm_w,
                           L0_attn_W_o, L0_ffn_weight, L0_ffn_bias)
               for b in range(B)]
    res = run_bass_kernel_spmd(nc, in_maps, core_ids=list(range(B)))
    return np.stack([res.results[b]["y"] for b in range(B)], axis=0).astype(np.float32)


# version 04
# speedup vs baseline: 1.0551x; 1.0551x over previous
import numpy as np
from contextlib import ExitStack

import concourse.bass as bass
import concourse.mybir as mybir
from concourse.bass_utils import run_bass_kernel_spmd
from concourse.alu_op_type import AluOpType as op

F32 = mybir.dt.float32
BF16 = mybir.dt.bfloat16
AF = mybir.ActivationFunctionType
AX = mybir.AxisListType.X

D = 2048
H = 16
C = 128
QKV_N = 3 * D
MAGIC = float(3 * 2 ** 22)
NEG = -60000.0
EPS = 1e-5
L2EPS = 1e-6
LN_HALF_DK = float(-0.5 * np.log(128.0))
LN2 = float(np.log(2.0))

ENGINES = ("pe", "act", "dve", "sp", "pool")
BLOCK_NAME = dict(pe="tensor", act="scalar", dve="vector", sp="sync", pool="gpsimd")


class P:

    NLANES = 4
    CHANNELS = ("pe", "act", "dve", "sp", "pool") + tuple(
        f"{e}_d{l}" for e in ("sp", "pool") for l in range(4))

    def __init__(self, nc, es):
        self.nc = nc
        self.sem = {ch: es.enter_context(nc.semaphore(f"sem_{ch}"))
                    for ch in self.CHANNELS}
        self.q = {e: [] for e in ENGINES}
        self.cnt = {ch: 0 for ch in self.CHANNELS}
        self.lane = {"sp": 0, "pool": 0}
        self.acc = {e: ([], []) for e in ENGINES}

    @staticmethod
    def _eng_of(ch):
        return ch.split("_")[0]

    @staticmethod
    def _ext(ap):
        if ap is None or isinstance(ap, (int, float)):
            return None
        try:
            lo = ap.offset
            hi = lo + 1
            for step, count in list(ap.ap)[1:]:
                hi += step * (count - 1)
            return (ap.tensor.name, lo, hi)
        except Exception:
            return (getattr(getattr(ap, "tensor", None), "name", "?"), 0, 1 << 60)

    @staticmethod
    def _hits(x, y):
        if x[0] == "__any__" or y[0] == "__any__":
            return True
        return x[0] == y[0] and x[1] < y[2] and y[1] < x[2]

    def _conflict(self, e, writes, reads):
        tw, tr = self.acc[e]
        for w in writes:
            if w is not None and any(self._hits(x, w) for x in tw + tr):
                return True
        for r in reads:
            if r is not None and any(self._hits(x, r) for x in tw):
                return True
        return False

    def _emit(self, e, fn, deps, inc, ch=None, drain=None, w=(), r=()):
        ch = ch or e
        if drain is None and e in ("act", "dve"):
            if w or r:
                writes = [self._ext(a) for a in w]
                reads = [self._ext(a) for a in r]
            else:
                writes = [("__any__", 0, 1 << 60)]
                reads = []
            drain = self._conflict(e, writes, reads) if (w or r) else True
            if drain:
                self.acc[e] = ([], [])
            self.acc[e][0].extend(x for x in writes if x)
            self.acc[e][1].extend(x for x in reads if x)
        if drain:
            self.q[e].append((lambda g: g.drain(), (), 0, ch))
        deps = tuple(d for d in deps
                     if d is not None and (self._eng_of(d[0]) != e or "_d" in d[0]))
        self.q[e].append((fn, deps, inc, ch))
        if inc:
            self.cnt[ch] += inc
            return (ch, self.cnt[ch])
        return None

    def raw(self, e, fn, deps=(), fence=True):
        return self._emit(e, fn, deps, 1 if fence else 0)

    def nopf(self, e, deps=()):
        if e in ("pe", "act", "dve"):
            return self._emit(e, lambda g: g.drain(), deps, 1, drain=False)
        return self._emit(e, lambda g: g.nop(), deps, 1, drain=False)

    def wait_self_dma(self, e, tok):
        sem = self.sem[tok[0]]
        v = tok[1]
        self.q[e].append((lambda g, sem=sem, v=v: g.wait_ge(sem, v) or None, (), 0, e))

    def dma_quiesce(self, e):
        deps = [(f"{e}_d{l}", self.cnt[f"{e}_d{l}"]) for l in range(self.NLANES)]
        return self.nopf(e, deps=deps)

    def barrier(self):
        t1 = [self.dma_quiesce(e) if e in ("sp", "pool") else self.nopf(e)
              for e in ENGINES]
        return [self.nopf(e, deps=t1) for e in ENGINES]

    def dma(self, e, outp, inp, deps=()):
        l = self.lane[e]
        self.lane[e] = (l + 1) % self.NLANES
        ch = f"{e}_d{l}"
        gate = (ch, self.cnt[ch])
        return self._emit(e, lambda g, o=outp, i=inp: g.dma_start(out=o, in_=i),
                          list(deps) + [gate], 16, ch=ch)

    def mm(self, dst, lhs, rhs, start, stop, deps=(), fence=False):
        return self._emit("pe", lambda t, d=dst, l=lhs, r=rhs, a=start, b=stop:
                          t.matmul(d, l, r, start=a, stop=b), deps,
                          1 if fence else 0)

    def tr(self, dst, src, ident, deps=(), fence=True):
        return self._emit("pe", lambda t, d=dst, s=src, i=ident:
                          t.transpose(d, s, i), deps, 1 if fence else 0)

    def cp(self, e, dst, src, deps=(), fence=True):
        return self._emit(e, (lambda v, d=dst, s=src: v.tensor_copy(d, s)) if e == "dve"
                          else (lambda s_, d=dst, s=src: s_.activation(
                              d, s, AF.Copy, bias=0.0, scale=1.0)),
                          deps, 1 if fence else 0, w=(dst,), r=(src,))

    def ts(self, dst, src, s1, s2, o1, o2=None, deps=(), fence=True, accum=None):
        def fn(v, d=dst, s=src, s1=s1, s2=s2, o1=o1, o2=o2, ac=accum):
            if o2 is None:
                return v.tensor_scalar(d, s, s1, None, o1, accum_out=ac) \
                    if ac is not None else v.tensor_scalar(d, s, s1, None, o1)
            return v.tensor_scalar(d, s, s1, s2, o1, o2, accum_out=ac) \
                if ac is not None else v.tensor_scalar(d, s, s1, s2, o1, o2)
        aps = lambda z: z if hasattr(z, "offset") else None
        return self._emit("dve", fn, deps, 1 if fence else 0,
                          w=(dst, accum), r=(src, aps(s1), aps(s2)))

    def tt(self, dst, a, b, o, deps=(), fence=True):
        return self._emit("dve", lambda v, d=dst, a=a, b=b, o=o:
                          v.tensor_tensor(d, a, b, o), deps, 1 if fence else 0,
                          w=(dst,), r=(a, b))

    def stt(self, dst, in0, sc, in1, o1, o2, deps=(), fence=True, accum=None):
        def fn(v, d=dst, i0=in0, sc=sc, i1=in1, o1=o1, o2=o2, ac=accum):
            if ac is not None:
                return v.scalar_tensor_tensor(d, i0, sc, i1, o1, o2, accum_out=ac)
            return v.scalar_tensor_tensor(d, i0, sc, i1, o1, o2)
        aps = lambda z: z if hasattr(z, "offset") else None
        return self._emit("dve", fn, deps, 1 if fence else 0,
                          w=(dst, accum), r=(in0, in1, aps(sc)))

    def act(self, dst, src, func, bias=0.0, scale=1.0, deps=(), fence=True, accum=None):
        def fn(s_, d=dst, s=src, f=func, b=bias, sc=scale, ac=accum):
            if ac is not None:
                return s_.activation(d, s, f, bias=b, scale=sc, accum_out=ac)
            return s_.activation(d, s, f, bias=b, scale=sc)
        aps = lambda z: z if hasattr(z, "offset") else None
        return self._emit("act", fn, deps, 1 if fence else 0,
                          w=(dst, accum), r=(src, aps(bias), aps(scale)))

    def red(self, dst, src, o, absv=False, deps=(), fence=True):
        return self._emit("dve", lambda v, d=dst, s=src, o=o, a=absv:
                          v.tensor_reduce(d, s, AX, o, apply_absolute_value=a),
                          deps, 1 if fence else 0, w=(dst,), r=(src,))

    def mset(self, e, ap, val, deps=(), fence=True):
        return self._emit(e, lambda g, a=ap, v=val: g.memset(a, v), deps,
                          1 if fence else 0, w=(ap,))

    def replay(self):
        with self.nc.Block() as block:
            for e in ENGINES:
                deco = getattr(block, BLOCK_NAME[e])
                items = self.q[e]
                sems = self.sem

                def body(eng, items=items):
                    last = {}
                    for fn, deps, inc, ch in items:
                        for (pch, v) in deps:
                            if last.get(pch, 0) < v:
                                eng.wait_ge(sems[pch], v)
                                last[pch] = v
                        inst = fn(eng)
                        if inc:
                            inst.then_inc(sems[ch], inc)
                deco(body)


def build_program(T=4096, dbg=False):
    nc = bass.Bass("TRN2", target_bir_lowering=False)
    NCH = T // C
    T1 = min(2048, T)
    NT1 = T // T1
    T3 = min(512, T)
    NT3 = T // T3
    kdbg = "ExternalOutput" if dbg else "Internal"

    x_in = nc.dram_tensor("x", [T, D], F32, kind="ExternalInput").ap()
    w_qkv = nc.dram_tensor("w_qkv", [D, QKV_N], F32, kind="ExternalInput").ap()
    w_ab = nc.dram_tensor("w_ab", [D, 2 * H + D], F32, kind="ExternalInput").ap()
    conv_w = nc.dram_tensor("conv_w", [QKV_N, 4], F32, kind="ExternalInput").ap()
    a_log = nc.dram_tensor("a_log", [H, 1], F32, kind="ExternalInput").ap()
    dt_bias = nc.dram_tensor("dt_bias", [H, 1], F32, kind="ExternalInput").ap()
    w_o = nc.dram_tensor("w_o", [D, D], F32, kind="ExternalInput").ap()
    ffn_w = nc.dram_tensor("ffn_w", [D, D], F32, kind="ExternalInput").ap()
    c_ident = nc.dram_tensor("c_ident", [128, 128], F32, kind="ExternalInput").ap()
    c_masks = nc.dram_tensor("c_masks", [128, 3 * 1024], F32, kind="ExternalInput").ap()
    c_ind = nc.dram_tensor("c_ind", [48, 3 * D], F32, kind="ExternalInput").ap()
    c_oh = nc.dram_tensor("c_oh", [128, H * H], F32, kind="ExternalInput").ap()
    c_wtile = nc.dram_tensor("c_wtile", [128, D], F32, kind="ExternalInput").ap()
    c_btile = nc.dram_tensor("c_btile", [128, D], F32, kind="ExternalInput").ap()
    y_out = nc.dram_tensor("y", [T, D], F32, kind="ExternalOutput").ap()

    qkc_s = nc.dram_tensor("qkc_s", [NCH, 48, 128, 128], BF16, kind=kdbg).ap()
    sg_s = nc.dram_tensor("sg_s", [T, D], BF16, kind=kdbg).ap()
    of_s = nc.dram_tensor("of_s", [NCH, H, 128, 128], BF16, kind=kdbg).ap()
    tern_s = nc.dram_tensor("tern_s", [D, D], BF16, kind=kdbg).ap()
    if dbg:
        gH_d = nc.dram_tensor("gH_d", [H, T], F32, kind=kdbg).ap()
        lnbH_d = nc.dram_tensor("lnbH_d", [H, T], F32, kind=kdbg).ap()
        alpha_d = nc.dram_tensor("alpha_d", [1, 4], F32, kind=kdbg).ap()

    es = ExitStack()
    SG = es.enter_context
    p = P(nc, es)
    es12 = ExitStack()
    SG12 = es12.enter_context

    ident_b = SG(nc.sbuf_tensor("ident_b", [128, 128], BF16)).ap()
    ident_f = SG(nc.sbuf_tensor("ident_f", [128, 128], F32)).ap()
    btile = SG(nc.sbuf_tensor("btile", [128, D], F32)).ap()
    nega = SG(nc.sbuf_tensor("nega", [H, 2], F32)).ap()
    dtb = SG(nc.sbuf_tensor("dtb", [H, 1], F32)).ap()
    ones1f = SG(nc.sbuf_tensor("ones1f", [1, 128], F32)).ap()
    alpha_t = SG(nc.sbuf_tensor("alpha_t", [128, 4], F32)).ap()
    csts = SG(nc.sbuf_tensor("csts", [128, 4], F32)).ap()
    masks_b = SG12(nc.sbuf_tensor("masks_b", [128, 3 * 1024], BF16)).ap()
    oh_b = SG12(nc.sbuf_tensor("oh_b", [128, H * H], BF16)).ap()
    wtile = SG12(nc.sbuf_tensor("wtile", [128, D], F32)).ap()
    gH = SG12(nc.sbuf_tensor("gH", [H, T], F32)).ap()
    lnbH = SG12(nc.sbuf_tensor("lnbH", [H, T], F32)).ap()
    ones16 = SG12(nc.sbuf_tensor("ones16", [H, 128], F32)).ap()

    tc = []
    tc.append(p.dma("pool", ident_b[:, :], c_ident[:, :]))
    tc.append(p.dma("sp", ident_f[:, :], c_ident[:, :]))
    tc.append(p.dma("pool", masks_b[:, :], c_masks[:, :]))
    tc.append(p.dma("pool", oh_b[:, :], c_oh[:, :]))
    tc.append(p.dma("sp", wtile[:, :], c_wtile[:, :]))
    tc.append(p.dma("sp", btile[:, :], c_btile[:, :]))
    tc.append(p.dma("sp", dtb[:, :], dt_bias[:, :]))
    tk_a = p.dma("sp", nega[:, 1:2], a_log[:, :])
    tk_na = p.act(nega[:, 0:1], nega[:, 1:2], AF.Exp, deps=[tk_a])
    tc.append(p.ts(nega[:, 0:1], nega[:, 0:1], -1.0, None, op.mult, deps=[tk_na]))
    p.mset("dve", ones16[:, :], 1.0)
    tc.append(p.mset("dve", ones1f[:, :], 1.0))
    p.mset("dve", csts[:, 0:1], L2EPS, fence=False)
    tc.append(p.mset("dve", csts[:, 1:2], EPS))
    nc.const_aps.register(F32, L2EPS, csts[:, 0:1])
    nc.const_aps.register(F32, EPS, csts[:, 1:2])
    TC = tuple(tc)

    es1 = ExitStack()
    S1 = es1.enter_context
    xf = S1(nc.sbuf_tensor("xf", [128, 16 * T1], BF16)).ap()
    x32 = S1(nc.sbuf_tensor("x32", [128, 2 * D], F32)).ap()
    xb16 = S1(nc.sbuf_tensor("xb16", [128, 2 * D], BF16)).ap()
    wqb = S1(nc.sbuf_tensor("wqb", [128, 2 * 2048], BF16)).ap()
    wabb = S1(nc.sbuf_tensor("wabb", [128, 16 * 512], BF16)).ap()
    wabsm = S1(nc.sbuf_tensor("wabsm", [128, 16 * 32], BF16)).ap()
    qpre = S1(nc.sbuf_tensor("qpre", [128, T1 + 4], BF16)).ap()
    tails = S1(nc.sbuf_tensor("tails", [128, 3 * 48], BF16)).ap()
    sq_out = S1(nc.sbuf_tensor("sq_out", [128, 2 * T1], BF16)).ap()
    sgate = S1(nc.sbuf_tensor("sgate", [128, 2 * 512], BF16)).ap()
    sab = S1(nc.sbuf_tensor("sab", [128, 32 * (T1 // 128)], F32)).ap()
    wcv = S1(nc.sbuf_tensor("wcv", [128, 192], F32)).ap()
    diag4 = S1(nc.sbuf_tensor("diag4", [128, 4 * 128], BF16)).ap()
    st1 = S1(nc.sbuf_tensor("st1", [H, 6 * 128], F32)).ap()
    ps_q = S1(nc.psum_tensor("ps_q", [128, 1024], F32)).ap()
    ps_ab = S1(nc.psum_tensor("ps_ab", [128, 2048], F32)).ap()
    ps_s1 = S1(nc.psum_tensor("ps_s1", [128, 512], F32)).ap()

    tk_wcv = p.dma("pool", wcv[:, :].rearrange("p (fb j) -> p fb j", j=4),
                   conv_w[:, :].rearrange("(fb p) j -> p fb j", p=128))
    tk_t0 = p.mset("dve", tails[:, :], 0.0)
    tail_tok = {fb: tk_t0 for fb in range(48)}

    NTB1 = T1 // 128
    SEGW = min(512, T1)
    NSEG = T1 // SEGW
    GSZ = min(4, NTB1)

    def softplus(out_ap, in_ap, tmp_ap, deps, scale=1.0):
        p.act(tmp_ap, in_ap, AF.Abs, scale=scale, deps=deps, fence=False)
        p.act(tmp_ap, tmp_ap, AF.Exp, scale=-1.0, fence=False)
        p.act(tmp_ap, tmp_ap, AF.Ln, bias=1.0, fence=False)
        t1 = p.act(out_ap, in_ap, AF.Relu, scale=scale, deps=deps)
        return p.tt(out_ap, out_ap, tmp_ap, op.add, deps=[t1])

    for ti in range(NT1):
        t0 = ti * T1
        tk_xtr = []
        tk_cst_s = {}
        tk_trm_s = {}
        for tb in range(NTB1):
            sl = tb % 2
            tk_x = p.dma("sp", x32[:, sl * D:(sl + 1) * D],
                         x_in[t0 + tb * 128: t0 + (tb + 1) * 128, :],
                         deps=[tk_cst_s.get(sl)])
            tk_cst = p.cp("act", xb16[:, sl * D:(sl + 1) * D],
                          x32[:, sl * D:(sl + 1) * D],
                          deps=[tk_x, tk_trm_s.get(sl)])
            tk_cst_s[sl] = tk_cst
            tkm = None
            for g4 in range(4):
                prev = tk_xtr[-1] if tk_xtr else None
                for j in range(4):
                    dblk = g4 * 4 + j
                    tkm = p.mm(ps_s1[:, j * 128:(j + 1) * 128],
                               xb16[:, sl * D + dblk * 128: sl * D + (dblk + 1) * 128],
                               ident_b[:, :], True, True,
                               deps=[tk_cst, prev] + list(TC), fence=(j == 3))
                tk_xtr.append(p.cp(
                    "dve",
                    xf[:, :].rearrange("p (b tt) -> p b tt", b=16)
                    [:, g4 * 4:(g4 + 1) * 4, tb * 128:(tb + 1) * 128],
                    ps_s1[:, :].rearrange("p (b tt) -> p b tt", b=4)[:, :, :],
                    deps=[tkm]))
            tk_trm_s[sl] = tkm
        tk_xf = p.nopf("dve", deps=tk_xtr)
        p.barrier()

        tk_wq = {}
        tk_fb_done = {}
        tk_sq = {}

        def wq_dma(fb, deps):
            sl = fb % 2
            return p.dma("pool",
                         wqb[:, sl * 2048:(sl + 1) * 2048].rearrange(
                             "p (kt n) -> p kt n", n=128),
                         w_qkv[:, fb * 128:(fb + 1) * 128].rearrange(
                             "(kt p) n -> p kt n", p=128),
                         deps=deps)
        tk_wq[0] = wq_dma(0, [tk_wcv])
        tk_wq[1] = wq_dma(1, [tk_wcv])
        for fb in range(48):
            sl = fb % 2
            tk_ev = []
            for seg in range(NSEG):
                pslot = (seg % 2) * 512
                tkm = None
                for kt in range(16):
                    tkm = p.mm(ps_q[:, pslot:pslot + SEGW],
                               wqb[:, sl * 2048 + kt * 128: sl * 2048 + (kt + 1) * 128],
                               xf[:, kt * T1 + seg * SEGW: kt * T1 + (seg + 1) * SEGW],
                               kt == 0, kt == 15,
                               deps=[tk_wq[fb], tk_xf, tk_fb_done.get(fb - 1),
                                     tk_ev[seg - 2] if seg >= 2 else None],
                               fence=(kt == 15))
                tk_ev.append(p.act(qpre[:, 3 + seg * SEGW: 3 + (seg + 1) * SEGW],
                                   ps_q[:, pslot:pslot + SEGW], AF.Copy, deps=[tkm]))
            if fb + 2 < 48:
                tk_wq[fb + 2] = wq_dma(fb + 2, [tkm, tk_wcv])
            tk_tl = p.cp("dve", qpre[:, 0:3], tails[:, fb * 3:(fb + 1) * 3],
                         deps=[tail_tok[fb], tk_fb_done.get(fb - 1)])
            tail_tok[fb] = p.cp("dve", tails[:, fb * 3:(fb + 1) * 3],
                                qpre[:, T1: T1 + 3], deps=[tk_ev[-1], tk_tl])
            tkd = None
            for j in range(4):
                tkd = p.ts(diag4[:, j * 128:(j + 1) * 128], ident_b[:, :],
                           wcv[:, fb * 4 + j: fb * 4 + j + 1], None, op.mult,
                           deps=[tk_wcv, tk_fb_done.get(fb - 1)], fence=(j == 3))
            tk_cv = []
            for seg in range(NSEG):
                pslot = (seg % 2) * 512
                tkm = None
                for j in range(4):
                    tkm = p.mm(ps_q[:, pslot:pslot + SEGW],
                               diag4[:, j * 128:(j + 1) * 128],
                               qpre[:, j + seg * SEGW: j + seg * SEGW + SEGW],
                               j == 0, j == 3,
                               deps=[tkd, tk_ev[-1], tk_tl,
                                     tk_cv[seg - 2] if seg >= 2 else None],
                               fence=(j == 3))
                tk_cv.append(p.act(sq_out[:, sl * T1 + seg * SEGW: sl * T1 + (seg + 1) * SEGW],
                                   ps_q[:, pslot:pslot + SEGW], AF.Silu,
                                   deps=[tkm, tk_sq.get(fb - 2)]))
            tk_fb_done[fb] = tk_cv[-1]
            if fb < 16:
                blk = fb * 3 + 1
            elif fb < 32:
                blk = (fb - 16) * 3
            else:
                blk = (fb - 32) * 3 + 2
            c0 = t0 // 128
            tk_sq[fb] = p.dma("pool",
                              qkc_s[c0:c0 + NTB1, blk, :, :].rearrange(
                                  "cc pp tt -> pp cc tt"),
                              sq_out[:, sl * T1:(sl + 1) * T1].rearrange(
                                  "pp (cc tt) -> pp cc tt", tt=128),
                              deps=tk_cv)
        p.barrier()

        tk_wsm = [p.dma("pool", wabsm[:, kt * 32:(kt + 1) * 32],
                        w_ab[kt * 128:(kt + 1) * 128, 0:32]) for kt in range(16)]
        tk_absm = []
        for tb in range(NTB1):
            tkm = None
            for kt in range(16):
                tkm = p.mm(ps_s1[:, 0:32],
                           xf[:, kt * T1 + tb * 128: kt * T1 + (tb + 1) * 128],
                           wabsm[:, kt * 32:(kt + 1) * 32],
                           kt == 0, kt == 15,
                           deps=tk_wsm + [tk_xf, tk_absm[tb - 1] if tb >= 1 else None],
                           fence=(kt == 15))
            tk_absm.append(p.cp("dve", sab[:, tb * 32:(tb + 1) * 32],
                                ps_s1[:, 0:32], deps=[tkm]))
        p.barrier()

        tk_gdma = {}
        tk_gmm = {}
        for nch in range(4):
            ncol0 = 32 + nch * 512
            tk_wab = p.dma("pool", wabb[:, :].rearrange("p (kt n) -> p kt n", n=512),
                           w_ab[:, ncol0:ncol0 + 512].rearrange(
                               "(kt p) n -> p kt n", p=128),
                           deps=[tk_gmm.get(nch - 1)])
            for tbg in range(NTB1 // GSZ):
                tk_mm4 = []
                for tb4 in range(GSZ):
                    tb = tbg * GSZ + tb4
                    prevkey = (nch, tbg - 1, tb4) if tbg > 0 else (nch - 1, NTB1 // GSZ - 1, tb4)
                    tkm = None
                    for kt in range(16):
                        tkm = p.mm(ps_ab[:, tb4 * 512:(tb4 + 1) * 512],
                                   xf[:, kt * T1 + tb * 128: kt * T1 + (tb + 1) * 128],
                                   wabb[:, kt * 512:(kt + 1) * 512],
                                   kt == 0, kt == 15,
                                   deps=[tk_wab, tk_xf, tk_gdma.get(prevkey)],
                                   fence=(kt == 15))
                    tk_mm4.append(tkm)
                tk_sd = []
                for tb4 in range(GSZ):
                    tb = tbg * GSZ + tb4
                    sl = tb4 % 2
                    tk_si = p.act(sgate[:, sl * 512:(sl + 1) * 512],
                                  ps_ab[:, tb4 * 512:(tb4 + 1) * 512], AF.Silu,
                                  deps=[tk_mm4[tb4],
                                        tk_sd[tb4 - 2] if tb4 >= 2 else None])
                    tk_sd.append(p.dma("sp",
                                       sg_s[t0 + tb * 128: t0 + (tb + 1) * 128,
                                            ncol0 - 32: ncol0 - 32 + 512],
                                       sgate[:, sl * 512:(sl + 1) * 512],
                                       deps=[tk_si]))
                    tk_gdma[(nch, tbg, tb4)] = tk_sd[-1]
                tk_gmm[nch] = tk_mm4[-1]
        p.barrier()

        aT = st1[:, 0:128]
        bbT = st1[:, 128:256]
        u_ = st1[:, 256:384]
        tmp_ = st1[:, 384:512]
        sp_ = st1[:, 512:640]
        tk_bg = None
        for tb in range(NTB1):
            tka1 = p.tr(ps_s1[0:16, 0:128], sab[:, tb * 32: tb * 32 + 16],
                        ident_f[:, :], deps=[tk_absm[tb], tk_bg])
            tka2 = p.tr(ps_s1[0:16, 128:256], sab[:, tb * 32 + 16: tb * 32 + 32],
                        ident_f[:, :], deps=[tka1])
            tc1 = p.cp("dve", aT, ps_s1[0:16, 0:128], deps=[tka1])
            tc2 = p.cp("dve", bbT, ps_s1[0:16, 128:256], deps=[tka2])
            tk_u = p.ts(u_, aT, dtb[:, 0:1], None, op.add, deps=[tc1])
            tsp = softplus(sp_, u_, tmp_, deps=[tk_u])
            tk_g = p.ts(gH[:, t0 + tb * 128: t0 + (tb + 1) * 128], sp_,
                        nega[:, 0:1], None, op.mult, deps=[tsp])
            tsp2 = softplus(sp_, bbT, tmp_, deps=[tc2, tk_g], scale=-1.0)
            tk_bg = p.ts(lnbH[:, t0 + tb * 128: t0 + (tb + 1) * 128], sp_,
                         -1.0, LN2, op.mult, op.add, deps=[tsp2])
        p.barrier()

    if dbg:
        p.dma("sp", gH_d[:, :], gH[:, :])
        p.dma("sp", lnbH_d[:, :], lnbH[:, :])
    p.barrier()
    es1.close()

    esf = ExitStack()
    SF = esf.enter_context
    fw = SF(nc.sbuf_tensor("fw", [128, 2 * D], F32)).ap()
    facc = SF(nc.sbuf_tensor("facc", [128, 16], F32)).ap()
    ftmp = SF(nc.sbuf_tensor("ftmp", [128, D], F32)).ap()
    ftrn = SF(nc.sbuf_tensor("ftrn", [128, D], BF16)).ap()
    ftrn2 = SF(nc.sbuf_tensor("ftrn2", [128, D], BF16)).ap()
    fsm = SF(nc.sbuf_tensor("fsm", [1, 16], F32)).ap()
    fcol = SF(nc.sbuf_tensor("fcol", [128, 1], F32)).ap()
    ps_f = SF(nc.psum_tensor("ps_f", [128, 1024], F32)).ap()

    tk_blk = []
    for rb in range(16):
        sl = rb % 2
        tk_l = p.dma("sp", fw[:, sl * D:(sl + 1) * D],
                     ffn_w[rb * 128:(rb + 1) * 128, :],
                     deps=[tk_blk[rb - 2] if rb >= 2 else None])
        tk_blk.append(p.red(facc[:, rb:rb + 1], fw[:, sl * D:(sl + 1) * D],
                            op.add, absv=True, deps=[tk_l]))
    tk_r = p.red(fcol[:, 0:1], facc[:, :], op.add, deps=tk_blk)
    tk_t = p.tr(ps_f[0:1, 0:128], fcol[:, 0:1], ident_f[:, :], deps=[tk_r])
    tk_r2 = p.red(fsm[0:1, 0:1], ps_f[0:1, 0:128], op.add, deps=[tk_t])
    p.ts(fsm[0:1, 1:2], fsm[0:1, 0:1], 1.0 / (D * D), EPS, op.mult, op.max,
         deps=[tk_r2], fence=False)
    p.raw("dve", lambda v: v.reciprocal(fsm[0:1, 2:3], fsm[0:1, 1:2]), fence=False)
    tk_al = p.ts(fsm[0:1, 3:4], fsm[0:1, 1:2], 1.0 / 127.0, None, op.mult)
    if dbg:
        p.dma("sp", alpha_d[:, :], fsm[0:1, 0:4], deps=[tk_al])
    tk_bc = p.mm(ps_f[:, 128:132], ones1f[:, :], fsm[0:1, 0:4], True, True,
                 deps=[tk_al, tk_t] + list(TC), fence=True)
    tk_at = p.cp("dve", alpha_t[:, 0:4], ps_f[:, 128:132], deps=[tk_bc])

    tk_pdma = None
    tk_rd = {}
    for rb in range(16):
        sl = rb % 2
        tk_l = p.dma("sp", fw[:, sl * D:(sl + 1) * D],
                     ffn_w[rb * 128:(rb + 1) * 128, :],
                     deps=[tk_at, tk_rd.get(rb - 2)])
        tk1 = p.act(ftmp[:, :], fw[:, sl * D:(sl + 1) * D], AF.Copy,
                    bias=MAGIC, scale=alpha_t[:, 2:3],
                    deps=[tk_l, tk_at, tk_rd.get(("ts", rb - 1))])
        tk_rd[rb] = tk1
        tk2 = p.act(ftmp[:, :], ftmp[:, :], AF.Copy, bias=-MAGIC, deps=[tk1])
        tk3 = p.ts(ftrn[:, :], ftmp[:, :], 1.0, -1.0, op.min, op.max,
                   deps=[tk2, tk_rd.get(("tr", rb - 1))])
        tk_rd[("ts", rb)] = tk3
        tk_cps = []
        for ib in range(16):
            poff = (ib % 2) * 512
            tkt = p.mm(ps_f[:, poff:poff + 128],
                       ftrn[:, ib * 128:(ib + 1) * 128], ident_b[:, :], True, True,
                       deps=[tk3, tk_cps[ib - 2] if ib >= 2 else None],
                       fence=True)
            tk_cps.append(p.cp("dve", ftrn2[:, ib * 128:(ib + 1) * 128],
                               ps_f[:, poff:poff + 128],
                               deps=[tkt, tk_rd.get(("dma", rb - 1, ib))]))
            tk_rd[("tr", rb)] = tkt
            tk_rd[("dma", rb, ib)] = p.dma(
                "pool", tern_s[ib * 128:(ib + 1) * 128, rb * 128:(rb + 1) * 128],
                ftrn2[:, ib * 128:(ib + 1) * 128], deps=[tk_cps[-1]])
        tk_pdma = tk_rd[("dma", rb, 15)]
    p.barrier()
    esf.close()

    _stage2(nc, p, es, T, NCH, qkc_s, sg_s, of_s, gH, lnbH, ident_b, ident_f,
            masks_b, oh_b, wtile, c_ind, ones16, ones1f)
    p.barrier()
    es12.close()

    _stage3(nc, p, es, T, T3, NT3, x_in, w_o, of_s, tern_s, y_out,
            ident_b, ident_f, btile, alpha_t)

    p.replay()
    es.close()
    return nc


def _stage2(nc, p, es, T, NCH, qkc_s, sg_s, of_s, gH, lnbH, ident_b, ident_f,
            masks_b, oh_b, wtile, c_ind, ones16, ones1f):
    es2 = ExitStack()
    SG = es2.enter_context
    qc = SG(nc.sbuf_tensor("qc", [128, 2 * 6144], BF16)).ap()
    sgc = SG(nc.sbuf_tensor("sgc", [128, 2 * D], BF16)).ap()
    S_sb = SG(nc.sbuf_tensor("S_sb", [128, H * 128], F32)).ap()
    S_bf = SG(nc.sbuf_tensor("S_bf", [128, H * 128], BF16)).ap()
    sqb = SG(nc.sbuf_tensor("sqb", [128, 2 * D], BF16)).ap()
    indb = SG(nc.sbuf_tensor("indb", [51, 3 * D], BF16)).ap()
    lA = SG(nc.sbuf_tensor("lA", [51, 128], BF16)).ap()
    lST = SG(nc.sbuf_tensor("lST", [51, 128], BF16)).ap()
    mst = SG(nc.sbuf_tensor("mst", [H, 8 * 128], F32)).ap()
    mpc = SG(nc.sbuf_tensor("mpc", [H, 9 * 128], BF16)).ap()
    lnssq = SG(nc.sbuf_tensor("lnssq", [H, 256], F32)).ap()
    cols = SG(nc.sbuf_tensor("cols", [128, 64], F32)).ap()
    expd = SG(nc.sbuf_tensor("expd", [128, 1024], F32)).ap()
    ktok = SG(nc.sbuf_tensor("ktok", [128, 1024], BF16)).ap()
    kdec = SG(nc.sbuf_tensor("kdec", [128, 2048], BF16)).ap()
    kff = SG(nc.sbuf_tensor("kff", [128, 1024], BF16)).ap()
    ntok = SG(nc.sbuf_tensor("ntok", [128, 2 * 1024], BF16)).ap()
    nbar = SG(nc.sbuf_tensor("nbar", [128, 2 * 1024], BF16)).ap()
    wch = SG(nc.sbuf_tensor("wch", [128, 2 * 1024], BF16)).ap()
    wbar = SG(nc.sbuf_tensor("wbar", [128, 2048], BF16)).ap()
    ub = SG(nc.sbuf_tensor("ub", [128, 2048], BF16)).ap()
    ost = SG(nc.sbuf_tensor("ost", [128, 2048], F32)).ap()
    ogt = SG(nc.sbuf_tensor("ogt", [128, 2048], F32)).ap()
    onb = SG(nc.sbuf_tensor("onb", [128, 2048], BF16)).ap()
    ofst = SG(nc.sbuf_tensor("ofst", [128, 2048], BF16)).ap()
    rms = SG(nc.sbuf_tensor("rms", [128, 48], F32)).ap()
    qst = SG(nc.sbuf_tensor("qst", [128, 1024], F32)).ap()
    grow = SG(nc.sbuf_tensor("grow", [1, 16], F32)).ap()
    gcb = SG(nc.sbuf_tensor("gcb", [128, 16], F32)).ap()
    PA = SG(nc.psum_tensor("PA", [128, 1024], F32)).ap()
    PB = SG(nc.psum_tensor("PB", [128, 1024], F32)).ap()
    PC = SG(nc.psum_tensor("PC", [128, 1024], F32)).ap()
    PD = SG(nc.psum_tensor("PD", [128, 1024], F32)).ap()

    p.mset("dve", S_sb[:, :], 0.0, fence=False)
    p.mset("dve", S_bf[:, :], 0.0, fence=False)
    p.mset("dve", lA[0:51, :], 1.0, fence=False)
    tk_init = p.mset("dve", lST[0:51, :], 1.0)
    tk_ind = p.dma("pool", indb[0:48, :], c_ind[:, :])

    tk_qc = {0: p.dma("sp", qc[:, 0:6144].rearrange("pp (b t) -> pp b t", t=128),
                      qkc_s[0, :, :, :].rearrange("b pp t -> pp b t"))}
    tk_sg = {0: p.dma("sp", sgc[:, 0:D], sg_s[0:128, :])}

    tk_Sw = {h: tk_init for h in range(H)}
    tk_used = tk_init
    MSL = dict(gc=0, m1=1, m2=2, m3=3, m4=4, tmp=5, exG=6)

    def msl(nm):
        i = MSL[nm]
        return mst[:, i * 128:(i + 1) * 128]

    for c in range(NCH):
        par = c % 2
        qcs = qc[:, par * 6144:(par + 1) * 6144]
        sgs = sgc[:, par * D:(par + 1) * D]
        if c + 1 < NCH:
            tk_qc[c + 1] = p.dma("sp",
                                 qc[:, (1 - par) * 6144:(2 - par) * 6144].rearrange(
                                     "pp (b t) -> pp b t", t=128),
                                 qkc_s[c + 1, :, :, :].rearrange("b pp t -> pp b t"),
                                 deps=[tk_used])
            tk_sg[c + 1] = p.dma("sp", sgc[:, (1 - par) * D:(2 - par) * D],
                                 sg_s[(c + 1) * 128:(c + 2) * 128, :],
                                 deps=[tk_used])

        tk_gc = p.raw("dve", lambda v, d=msl("gc"), o=ones16[:, :],
                      g_=gH[:, c * 128:(c + 1) * 128]:
                      v.tensor_tensor_scan(d, o, g_, 0.0, op.mult, op.add),
                      deps=[tk_init, tk_used])
        tk_k2 = p.act(sqb[:, :].rearrange("p (h kq t) -> p h kq t", h=H, kq=2)[:, :, 0, :],
                      qcs.rearrange("p (h z t) -> p h z t", h=H, z=3)[:, :, 0, :],
                      AF.Square, deps=[tk_qc[c], tk_used])
        tk_q2 = p.act(sqb[:, :].rearrange("p (h kq t) -> p h kq t", h=H, kq=2)[:, :, 1, :],
                      qcs.rearrange("p (h z t) -> p h z t", h=H, z=3)[:, :, 1, :],
                      AF.Square, deps=[tk_qc[c]])
        tkm = None
        for h in range(H):
            tkm = p.mm(PD[0:16, 0:256], oh_b[:, h * 16:(h + 1) * 16],
                       sqb[:, h * 256:(h + 1) * 256], h == 0, h == 15,
                       deps=[tk_k2, tk_q2, tk_used], fence=(h == 15))
        tk_ln = p.act(lnssq[:, :], PD[0:16, 0:256], AF.Ln, bias=L2EPS, deps=[tkm])
        p.stt(msl("m1"), lnssq[:, 0:128], -0.5, msl("gc"), op.mult, op.add,
              deps=[tk_ln, tk_gc], fence=False)
        p.stt(msl("tmp"), lnssq[:, 0:128], -0.5, lnbH[:, c * 128:(c + 1) * 128],
              op.mult, op.add, deps=[tk_ln], fence=False)
        p.tt(msl("m2"), msl("tmp"), msl("gc"), op.subtract, fence=False)
        p.stt(msl("tmp"), lnssq[:, 128:256], -0.5, msl("gc"), op.mult, op.add,
              fence=False)
        p.ts(msl("m3"), msl("tmp"), LN_HALF_DK, None, op.add, fence=False)
        tk_m = p.ts(msl("m4"), msl("m2"), msl("gc")[:, 127:128], None, op.add)
        tk_pc = []
        for mi, nm in enumerate(("m1", "m2", "m3")):
            b0 = mi * 3
            p.cp("dve", mpc[:, b0 * 128:(b0 + 1) * 128], msl(nm), deps=[tk_m],
                 fence=False)
            p.tt(msl("tmp"), msl(nm), mpc[:, b0 * 128:(b0 + 1) * 128], op.subtract,
                 fence=False)
            p.cp("dve", mpc[:, (b0 + 1) * 128:(b0 + 2) * 128], msl("tmp"), fence=False)
            p.tt(msl("tmp"), msl("tmp"), mpc[:, (b0 + 1) * 128:(b0 + 2) * 128],
                 op.subtract, fence=False)
            tk_pc.append(p.cp("dve", mpc[:, (b0 + 2) * 128:(b0 + 3) * 128], msl("tmp")))
        tk_pc = p.nopf("dve", deps=tk_pc)
        tk_asm = []
        for j in range(3):
            tk_asm.append(p.dma("sp", lA[16 * j:16 * (j + 1), :],
                                mpc[:, j * 128:(j + 1) * 128],
                                deps=[tk_pc, tk_used]))
            tk_asm.append(p.dma("sp", lST[16 * j:16 * (j + 1), :],
                                mpc[:, (3 + j) * 128:(4 + j) * 128],
                                deps=[tk_pc, tk_used]))
            for orient, mb in ((0, 3), (1, 0), (2, 6)):
                tk_asm.append(p.dma("sp",
                                    indb[48 + j:49 + j, orient * D:orient * D + D],
                                    mpc[:, (mb + j) * 128:(mb + j + 1) * 128],
                                    deps=[tk_pc, tk_used, tk_ind]))
        tk_tr1 = p.tr(PD[:, 256:272], msl("m1"), ident_f[0:16, 0:16], deps=[tk_m, tkm])
        tk_tr2 = p.tr(PD[:, 272:288], msl("m4"), ident_f[0:16, 0:16], deps=[tk_m])
        tk_tr3 = p.tr(PD[:, 288:304], msl("m3"), ident_f[0:16, 0:16], deps=[tk_m])
        tk_cols = p.act(cols[:, 0:48], PD[:, 256:304], AF.Exp,
                        deps=[tk_tr1, tk_tr2, tk_tr3])
        tk_xg = p.act(msl("exG")[:, 0:1], msl("gc")[:, 127:128], AF.Exp, deps=[tk_gc])
        tk_eg = p.tr(PD[0:1, 304:320], msl("exG")[:, 0:1], ident_f[0:16, 0:16],
                     deps=[tk_xg, tk_cols])
        tk_egc = p.cp("dve", grow[0:1, 0:16], PD[0:1, 304:320], deps=[tk_eg])
        tk_gcb = p.mm(PD[:, 320:336], ones1f[:, :], grow[0:1, 0:16], True, True,
                      deps=[tk_egc], fence=True)
        tk_gcb = p.cp("dve", gcb[:, 0:16], PD[:, 320:336], deps=[tk_gcb])

        tk_of_dma = []
        tk_Send = []
        for g in range(2):
            h0 = g * 8

            def kqv(hh):
                b = ((h0 + hh) * 3) * 128
                return (qcs[:, b + 128:b + 256], qcs[:, b:b + 128],
                        qcs[:, b + 256:b + 384])

            tkm = None
            for hh in range(8):
                q_, k_, v_ = kqv(hh)
                tkm = p.mm(PA[:, hh * 128:(hh + 1) * 128], k_, ident_b[:, :],
                           True, True,
                           deps=[tk_qc[c], tk_cols, tk_gcb, tk_Sw[(h0 - 1) % H]],
                           fence=(hh == 7))
            tk_ksc = []
            for hh in range(8):
                h = h0 + hh
                tk_ksc.append(p.ts(ktok[:, hh * 128:(hh + 1) * 128],
                                   PA[:, hh * 128:(hh + 1) * 128],
                                   cols[:, h:h + 1], -1.0, op.mult, op.mult,
                                   deps=[tkm]))
                tk_ksc.append(p.ts(kdec[:, h * 128:(h + 1) * 128],
                                   PA[:, hh * 128:(hh + 1) * 128],
                                   cols[:, 16 + h:16 + h + 1], None, op.mult,
                                   deps=[tkm]))
            tk_ksc = p.nopf("dve", deps=tk_ksc)
            tkm = None
            for hh in range(8):
                tkm = p.mm(PB[:, hh * 128:(hh + 1) * 128],
                           ktok[:, hh * 128:(hh + 1) * 128], ident_b[:, :],
                           True, True, deps=[tk_ksc], fence=(hh == 7))
            tk_kff = p.cp("dve", kff[:, :], PB[:, :], deps=[tkm])
            tkm = None
            for hh in range(8):
                q_, k_, v_ = kqv(hh)
                tkm = p.mm(PC[:, hh * 128:(hh + 1) * 128], v_, ident_b[:, :],
                           hh % 4 == 0, False, deps=[tk_qc[c], tk_kff],
                           fence=False)
            for hh in range(8):
                h = h0 + hh
                tkm = p.mm(PC[:, hh * 128:(hh + 1) * 128],
                           kff[:, hh * 128:(hh + 1) * 128],
                           S_bf[:, h * 128:(h + 1) * 128], False, hh % 4 == 3,
                           deps=[tk_kff, tk_Sw[h]], fence=(hh == 7))
            tk_w0 = p.cp("dve", wch[:, 0:1024], PC[:, :], deps=[tkm])
            tkm = None
            for hh in range(8):
                q_, k_, v_ = kqv(hh)
                dst = PA if hh < 4 else PB
                off = (hh % 4) * 256
                b = ((h0 + hh) * 3) * 128
                tkm = p.mm(dst[:, off:off + 256], k_, qcs[:, b:b + 256],
                           True, True, deps=[tk_kff, tk_w0, tk_qc[c]],
                           fence=(hh == 7))
            tk_gram = tkm

            def diff_build(orient, mask_i, out_sb, sign, kq, deps):
                lhs = lA if orient == 0 else lST
                tkk = None
                for half in range(2):
                    tkk = p.mm(PD[:, half * 512:(half + 1) * 512], lhs[:, :],
                               indb[:, orient * D + g * 1024 + half * 512:
                                    orient * D + g * 1024 + (half + 1) * 512],
                               True, False,
                               deps=list(deps) + tk_asm + [tk_ind], fence=False)
                    tkk = p.mm(PD[:, half * 512:(half + 1) * 512], ident_b[:, :],
                               masks_b[:, mask_i * 1024 + half * 512:
                                       mask_i * 1024 + (half + 1) * 512],
                               False, True, deps=[], fence=(half == 1))
                tke = p.act(expd[:, :], PD[:, :], AF.Exp, deps=[tkk] + list(deps))
                tks = []
                for half in range(2):
                    dst = PA if half == 0 else PB
                    tks.append(p.stt(
                        out_sb.rearrange("p (hh s) -> p hh s", hh=8)
                        [:, half * 4:(half + 1) * 4, :],
                        expd[:, :].rearrange("p (hh s) -> p hh s", hh=8)
                        [:, half * 4:(half + 1) * 4, :],
                        float(sign),
                        dst[:, :].rearrange("p (hh kq s) -> p hh kq s", hh=4, kq=2)
                        [:, :, kq, :],
                        op.mult, op.mult, deps=[tke, tk_gram]))
                return p.nopf("dve", deps=tks), tke

            tk_n0, tke0 = diff_build(0, 0, ntok[:, 0:1024], -1.0, 0, [tk_gram])
            tk_nb0, tke1 = diff_build(1, 1, nbar[:, 0:1024], -1.0, 0, [tke0, tk_n0])
            tk_wb, tke2 = diff_build(2, 2, wbar[:, h0 * 128:(h0 + 8) * 128],
                                     1.0, 1, [tke1, tk_nb0])

            tk_wcur, tk_ncur, tk_bcur = tk_w0, tk_n0, tk_nb0
            cur = 0
            for lev in range(7):
                tkm = None
                for hh in range(8):
                    tkm = p.mm(PC[:, hh * 128:(hh + 1) * 128],
                               nbar[:, cur * 1024 + hh * 128: cur * 1024 + (hh + 1) * 128],
                               wch[:, (lev % 2) * 1024 + hh * 128:
                                   (lev % 2) * 1024 + (hh + 1) * 128],
                               True, True, deps=[tk_wcur, tk_bcur, tk_wb],
                               fence=(hh == 7))
                tk_wcur = p.tt(wch[:, ((lev + 1) % 2) * 1024:
                                   ((lev + 1) % 2) * 1024 + 1024],
                               wch[:, (lev % 2) * 1024:(lev % 2) * 1024 + 1024],
                               PC[:, :], op.add, deps=[tkm])
                if lev < 6:
                    tks = None
                    for hh in range(8):
                        tks = p.mm(PA[:, hh * 128:(hh + 1) * 128],
                                   nbar[:, cur * 1024 + hh * 128:
                                        cur * 1024 + (hh + 1) * 128],
                                   ntok[:, cur * 1024 + hh * 128:
                                        cur * 1024 + (hh + 1) * 128],
                                   True, True,
                                   deps=[tk_ncur, tk_bcur, tk_wb, tk_n0],
                                   fence=False)
                    for hh in range(8):
                        tks = p.mm(PB[:, hh * 128:(hh + 1) * 128],
                                   ntok[:, cur * 1024 + hh * 128:
                                        cur * 1024 + (hh + 1) * 128],
                                   nbar[:, cur * 1024 + hh * 128:
                                        cur * 1024 + (hh + 1) * 128],
                                   True, True, deps=[], fence=(hh == 7))
                    nxt = 1 - cur
                    tk_ncur = p.cp("dve", ntok[:, nxt * 1024:(nxt + 1) * 1024],
                                   PA[:, :], deps=[tks])
                    tk_bcur = p.cp("dve", nbar[:, nxt * 1024:(nxt + 1) * 1024],
                                   PB[:, :], deps=[tks])
                    cur = nxt
            tk_U = p.cp("dve", ub[:, g * 1024:(g + 1) * 1024],
                        wch[:, (7 % 2) * 1024:(7 % 2) * 1024 + 1024], deps=[tk_wcur])

            tkm = None
            for hh in range(8):
                q_, k_, v_ = kqv(hh)
                h = h0 + hh
                tkm = p.mm(PA[:, hh * 128:(hh + 1) * 128], q_,
                           S_bf[:, h * 128:(h + 1) * 128], True, True,
                           deps=[tk_U, tk_Sw[h], tk_ncur, tk_bcur], fence=False)
            for hh in range(8):
                h = h0 + hh
                tkm = p.mm(PB[:, hh * 128:(hh + 1) * 128],
                           wbar[:, h * 128:(h + 1) * 128],
                           ub[:, h * 128:(h + 1) * 128], True, True,
                           deps=[tk_U, tk_wb, tk_bcur], fence=(hh == 7))
            tk_o = tkm
            tk_qs = []
            for hh in range(8):
                h = h0 + hh
                tk_qs.append(p.ts(qst[:, hh * 128:(hh + 1) * 128],
                                  PA[:, hh * 128:(hh + 1) * 128],
                                  cols[:, 32 + h:32 + h + 1], None, op.mult,
                                  deps=[tk_o]))
            tk_ov = p.stt(ost[:, g * 1024:(g + 1) * 1024], qst[:, :], 1.0,
                          PB[:, :], op.mult, op.add, deps=tk_qs + [tk_o])
            tk_og = p.tt(ogt[:, g * 1024:(g + 1) * 1024],
                         ost[:, g * 1024:(g + 1) * 1024],
                         sgs[:, g * 1024:(g + 1) * 1024], op.mult,
                         deps=[tk_ov, tk_sg[c]])
            tk_s2 = p.act(ost[:, g * 1024:(g + 1) * 1024],
                          ogt[:, g * 1024:(g + 1) * 1024], AF.Square, deps=[tk_og])
            tk_red = p.red(rms[:, g * 8:(g + 1) * 8],
                           ost[:, :].rearrange("p (hh s) -> p hh s", hh=16)
                           [:, g * 8:(g + 1) * 8, :], op.add, deps=[tk_s2])
            p.act(rms[:, 16 + g * 8:16 + (g + 1) * 8], rms[:, g * 8:(g + 1) * 8],
                  AF.Ln, bias=EPS, scale=1.0 / 128.0, deps=[tk_red], fence=False)
            tk_rs = p.act(rms[:, 32 + g * 8:32 + (g + 1) * 8],
                          rms[:, 16 + g * 8:16 + (g + 1) * 8], AF.Exp, scale=-0.5)
            tk_on = []
            for hh in range(8):
                h = h0 + hh
                tk_on.append(p.stt(onb[:, h * 128:(h + 1) * 128],
                                   ogt[:, g * 1024 + hh * 128:
                                       g * 1024 + (hh + 1) * 128],
                                   rms[:, 32 + g * 8 + hh:32 + g * 8 + hh + 1],
                                   wtile[:, h * 128:(h + 1) * 128],
                                   op.mult, op.mult, deps=[tk_rs, tk_og]))
            tkm = None
            for hh in range(8):
                h = h0 + hh
                tkm = p.mm(PC[:, hh * 128:(hh + 1) * 128],
                           onb[:, h * 128:(h + 1) * 128], ident_b[:, :],
                           True, True, deps=[tk_on[hh], tk_U], fence=(hh == 7))
            tk_ofe = p.cp("dve", ofst[:, g * 1024:(g + 1) * 1024], PC[:, :],
                          deps=[tkm])
            tk_of_dma.append(p.dma("pool",
                                   of_s[c, h0:h0 + 8, :, :].rearrange(
                                       "hh pp tt -> pp hh tt"),
                                   ofst[:, g * 1024:(g + 1) * 1024].rearrange(
                                       "pp (hh tt) -> pp hh tt", tt=128),
                                   deps=[tk_ofe]))
            tkm = None
            for hh in range(8):
                h = h0 + hh
                tkm = p.mm(PA[:, hh * 128:(hh + 1) * 128],
                           kdec[:, h * 128:(h + 1) * 128],
                           ub[:, h * 128:(h + 1) * 128], True, True,
                           deps=[tk_U, tk_ksc] + tk_qs, fence=(hh == 7))
            for hh in range(8):
                h = h0 + hh
                tk_su = p.stt(S_sb[:, h * 128:(h + 1) * 128],
                              S_sb[:, h * 128:(h + 1) * 128],
                              gcb[:, h:h + 1], PA[:, hh * 128:(hh + 1) * 128],
                              op.mult, op.add, deps=[tkm, tk_gcb])
                tk_Sw[h] = p.cp("dve", S_bf[:, h * 128:(h + 1) * 128],
                                S_sb[:, h * 128:(h + 1) * 128], deps=[tk_su])
                tk_Send.append(tk_Sw[h])
        tk_used = p.nopf("dve", deps=tk_of_dma + tk_Send)
    es2.close()


def _stage3(nc, p, es, T, T3, NT3, x_in, w_o, of_s, tern_s, y_out,
            ident_b, ident_f, btile, alpha_t):
    SG = es.enter_context
    wo_sb = SG(nc.sbuf_tensor("wo_sb", [128, 16 * D], BF16)).ap()
    tern_sb = SG(nc.sbuf_tensor("tern_sb", [128, 16 * D], BF16)).ap()
    ofsb = SG(nc.sbuf_tensor("ofsb", [128, 2 * D], BF16)).ap()
    xin = SG(nc.sbuf_tensor("xin", [128, D], F32)).ap()
    xr = SG(nc.sbuf_tensor("xr", [128, D], F32)).ap()
    scr = SG(nc.sbuf_tensor("scr", [128, D], F32)).ap()
    yb = SG(nc.sbuf_tensor("yb", [128, D], F32)).ap()
    qx = SG(nc.sbuf_tensor("qx", [128, D], BF16)).ap()
    qxf = SG(nc.sbuf_tensor("qxf", [128, D], BF16)).ap()
    zb = SG(nc.sbuf_tensor("zb", [128, D], F32)).ap()
    outb = SG(nc.sbuf_tensor("outb", [128, D], F32)).ap()
    stats = SG(nc.sbuf_tensor("stats", [128, 16], F32)).ap()
    psA = SG(nc.psum_tensor("psA", [128, 2048], F32)).ap()
    psB = SG(nc.psum_tensor("psB", [128, 2048], F32)).ap()

    tk_wo = [p.dma("pool", wo_sb[:, kt * D:(kt + 1) * D],
                   w_o[kt * 128:(kt + 1) * 128, :]) for kt in range(16)]
    tk_tern = [p.dma("sp", tern_sb[:, kt * D:(kt + 1) * D],
                     tern_s[kt * 128:(kt + 1) * 128, :]) for kt in range(16)]
    tk_out = []
    prev = {}
    for ti in range(NT3):
        t0 = ti * T3
        c0 = t0 // 128
        ntb = T3 // 128
        tk_of = {0: p.dma("sp",
                          ofsb[:, 0:D].rearrange("pp (h tt) -> pp h tt", tt=128),
                          of_s[c0, :, :, :].rearrange("h pp tt -> pp h tt"),
                          deps=[prev.get(("tile_pe", ti - 1))])}
        for tb in range(ntb):
            sl = tb % 2
            if tb + 1 < ntb:
                tk_of[tb + 1] = p.dma(
                    "sp",
                    ofsb[:, ((tb + 1) % 2) * D:((tb + 1) % 2 + 1) * D].rearrange(
                        "pp (h tt) -> pp h tt", tt=128),
                    of_s[c0 + tb + 1, :, :, :].rearrange("h pp tt -> pp h tt"),
                    deps=[prev.get(("attnpe", (ti, tb - 1))),
                          prev.get(("tile_pe", ti - 1))])
            tk_x = p.dma("sp", xin[:, :],
                         x_in[t0 + tb * 128: t0 + (tb + 1) * 128, :],
                         deps=[prev.get(("s1", (ti, tb - 1)))])
            tkm = None
            for oc in range(4):
                for kt in range(16):
                    tkm = p.mm(psA[:, oc * 512:(oc + 1) * 512],
                               ofsb[:, sl * D + kt * 128: sl * D + (kt + 1) * 128],
                               wo_sb[:, kt * D + oc * 512: kt * D + (oc + 1) * 512],
                               kt == 0, kt == 15,
                               deps=[tk_of[tb], prev.get(("qxf", (ti, tb - 1)))] + tk_wo,
                               fence=(oc == 3 and kt == 15))
            tk_attn = tkm
            prev[("attnpe", (ti, tb))] = tk_attn
            tk_s1 = p.stt(xr[:, :], xin[:, :], 1.0, psA[:, :],
                          op.mult, op.add, deps=[tk_attn, tk_x],
                          accum=stats[:, 0:1])
            prev[("s1", (ti, tb))] = tk_s1
            tk_s2 = p.act(scr[:, :], xr[:, :], AF.Square, deps=[tk_s1],
                          accum=stats[:, 1:2])
            p.ts(stats[:, 2:3], stats[:, 0:1], 1.0 / D, None, op.mult,
                 deps=[tk_s2], fence=False)
            p.tt(stats[:, 3:4], stats[:, 2:3], stats[:, 2:3], op.mult, fence=False)
            tk_var = p.stt(stats[:, 4:5], stats[:, 1:2], 1.0 / D, stats[:, 3:4],
                           op.mult, op.subtract)
            p.act(stats[:, 5:6], stats[:, 4:5], AF.Ln, bias=EPS, deps=[tk_var],
                  fence=False)
            tk_st = p.act(stats[:, 6:7], stats[:, 5:6], AF.Exp, scale=-0.5)
            tk_y = p.ts(yb[:, :], xr[:, :], stats[:, 2:3], stats[:, 6:7],
                        op.subtract, op.mult, deps=[tk_st])
            tk_am = p.red(stats[:, 7:8], yb[:, :], op.max, absv=True, deps=[tk_y])
            p.ts(stats[:, 8:9], stats[:, 7:8], EPS, None, op.max, deps=[tk_am],
                 fence=False)
            p.raw("dve", lambda v: v.reciprocal(stats[:, 9:10], stats[:, 8:9]),
                  fence=False)
            p.ts(stats[:, 10:11], stats[:, 9:10], 127.0, None, op.mult, fence=False)
            tk_so = p.tt(stats[:, 11:12], stats[:, 8:9], alpha_t[:, 3:4], op.mult)
            tk_q1 = p.act(scr[:, :], yb[:, :], AF.Copy, bias=MAGIC,
                          scale=stats[:, 10:11], deps=[tk_so])
            tk_q2 = p.act(qx[:, :], scr[:, :], AF.Copy, bias=-MAGIC, deps=[tk_q1])
            tk_tr = []
            for g4 in range(4):
                poff = g4 * 512
                tkm = None
                for j in range(4):
                    dblk = g4 * 4 + j
                    tkm = p.mm(psA[:, poff + j * 128: poff + (j + 1) * 128],
                               qx[:, dblk * 128:(dblk + 1) * 128], ident_b[:, :],
                               True, True, deps=[tk_q2, tk_s1,
                                                 tk_tr[g4 - 2] if g4 >= 2 else None],
                               fence=(j == 3))
                tk_tr.append(p.cp("dve", qxf[:, g4 * 512:(g4 + 1) * 512],
                                  psA[:, poff:poff + 512], deps=[tkm]))
            tk_qxf = p.nopf("dve", deps=tk_tr)
            prev[("qxf", (ti, tb))] = tk_qxf
            tk_f = []
            for oc in range(4):
                tkm = None
                for kt in range(16):
                    tkm = p.mm(psB[:, oc * 512:(oc + 1) * 512],
                               qxf[:, kt * 128:(kt + 1) * 128],
                               tern_sb[:, kt * D + oc * 512: kt * D + (oc + 1) * 512],
                               kt == 0, kt == 15,
                               deps=[tk_qxf, prev.get(("zr", (ti, tb - 1)))] + tk_tern,
                               fence=(kt == 15))
                tk_f.append(tkm)
            prev[("tile_pe", ti)] = tk_f[-1]
            tk_z = []
            for oc in range(4):
                tk_z.append(p.stt(zb[:, oc * 512:(oc + 1) * 512],
                                  psB[:, oc * 512:(oc + 1) * 512], stats[:, 11:12],
                                  btile[:, oc * 512:(oc + 1) * 512],
                                  op.mult, op.add, deps=[tk_f[oc], tk_so]))
            tk_zr = p.tt(zb[:, :], zb[:, :], yb[:, :], op.add, deps=tk_z)
            prev[("zr", (ti, tb))] = tk_zr
            tk_c1 = p.act(scr[:, :], zb[:, :], AF.Copy, deps=[tk_zr],
                          accum=stats[:, 0:1])
            tk_c2 = p.act(scr[:, :], zb[:, :], AF.Square, deps=[tk_c1],
                          accum=stats[:, 1:2])
            p.ts(stats[:, 2:3], stats[:, 0:1], 1.0 / D, None, op.mult,
                 deps=[tk_c2], fence=False)
            p.tt(stats[:, 3:4], stats[:, 2:3], stats[:, 2:3], op.mult, fence=False)
            tk_v2 = p.stt(stats[:, 4:5], stats[:, 1:2], 1.0 / D, stats[:, 3:4],
                          op.mult, op.subtract)
            p.act(stats[:, 5:6], stats[:, 4:5], AF.Ln, bias=EPS, deps=[tk_v2],
                  fence=False)
            tk_st2 = p.act(stats[:, 6:7], stats[:, 5:6], AF.Exp, scale=-0.5)
            tk_o = p.ts(outb[:, :], zb[:, :], stats[:, 2:3],
                        stats[:, 6:7], op.subtract, op.mult,
                        deps=[tk_st2, tk_out[-1] if tk_out else None])
            prev[("ln1", (ti, tb))] = tk_o
            tk_out.append(p.dma("sp", y_out[t0 + tb * 128: t0 + (tb + 1) * 128, :],
                                outb[:, :], deps=[tk_o]))

    p.wait_self_dma("sp", tk_out[-1])


def make_aux():
    ident = np.eye(128, dtype=np.float32)
    mA = np.where(np.tril(np.ones((128, 128)), -1) > 0, 0.0, NEG).astype(np.float32)
    mNb = np.where(np.triu(np.ones((128, 128)), 1) > 0, 0.0, NEG).astype(np.float32)
    mW = np.where(np.triu(np.ones((128, 128)), 0) > 0, 0.0, NEG).astype(np.float32)
    masks = np.concatenate([np.tile(m, (1, 8)) for m in (mA, mNb, mW)], axis=1)
    ind = np.zeros((48, D), np.float32)
    for j in range(3):
        for h in range(H):
            ind[j * 16 + h, h * 128:(h + 1) * 128] = 1.0
    ind3 = np.tile(ind, (1, 3))
    oh = np.zeros((128, H * H), np.float32)
    for h in range(H):
        oh[:, h * 16 + h] = 1.0
    return ident, masks, ind3, oh


def make_inputs(xb, W_qkv, W_ab, conv_w, A_log, dt_bias, o_norm_w, W_o, fw, fb):
    ident, masks, ind3, oh = make_aux()
    wt = np.ascontiguousarray(
        np.broadcast_to(np.tile(np.asarray(o_norm_w, np.float32), H), (128, D)))
    bt = np.ascontiguousarray(
        np.broadcast_to(np.asarray(fb, np.float32), (128, D)))
    return dict(
        x=np.ascontiguousarray(xb, np.float32),
        w_qkv=np.ascontiguousarray(W_qkv, np.float32),
        w_ab=np.ascontiguousarray(W_ab, np.float32),
        conv_w=np.ascontiguousarray(conv_w, np.float32),
        a_log=np.asarray(A_log, np.float32).reshape(H, 1),
        dt_bias=np.asarray(dt_bias, np.float32).reshape(H, 1),
        w_o=np.ascontiguousarray(W_o, np.float32),
        ffn_w=np.ascontiguousarray(fw, np.float32),
        c_ident=ident, c_masks=masks, c_ind=ind3, c_oh=oh,
        c_wtile=wt, c_btile=bt,
    )


def kernel(L0_attn_hidden_states, L0_attn_W_qkv, L0_attn_W_ab, L0_attn_conv_w,
           L0_attn_A_log, L0_attn_dt_bias, L0_attn_o_norm_w, L0_attn_W_o,
           L0_ffn_weight, L0_ffn_bias):
    x = np.ascontiguousarray(L0_attn_hidden_states, dtype=np.float32)
    B, T, _ = x.shape
    nc = build_program(T=T)
    in_maps = [make_inputs(x[b], L0_attn_W_qkv, L0_attn_W_ab, L0_attn_conv_w,
                           L0_attn_A_log, L0_attn_dt_bias, L0_attn_o_norm_w,
                           L0_attn_W_o, L0_ffn_weight, L0_ffn_bias)
               for b in range(B)]
    res = run_bass_kernel_spmd(nc, in_maps, core_ids=list(range(B)))
    return np.stack([res.results[b]["y"] for b in range(B)], axis=0).astype(np.float32)
